# version 11
# baseline (speedup 1.0000x reference)
import numpy as np
import ml_dtypes
import concourse.bass as bass
import concourse.mybir as mybir
from concourse.bass_utils import run_bass_kernel_spmd
from contextlib import ExitStack

F32 = mybir.dt.float32
BF16 = mybir.dt.bfloat16
I16 = mybir.dt.int16
AF = mybir.ActivationFunctionType
ALU = mybir.AluOpType
AX = mybir.AxisListType

ENGINES = ("sync", "tensor", "vector", "scalar", "gpsimd")

NB, CD, NH = 8, 32, 16
HD = CD * NB
DM = 4096
BT = 4096
QT = 1024
II = 11008
GRP = 128
NG_GU = DM // GRP
NG_D = II // GRP
SCALE = 1.0 / np.sqrt(HD)
EPS = 1e-5


def _ga_tables():
    n = 8
    masks = sorted(range(n), key=lambda m: (bin(m).count("1"), m))
    pos = {m: k for k, m in enumerate(masks)}
    C = np.zeros((n, n, n), dtype=np.float64)
    for ai, a in enumerate(masks):
        for bi, b in enumerate(masks):
            swaps, at = 0, a >> 1
            while at:
                swaps += bin(at & b).count("1")
                at >>= 1
            C[ai, bi, pos[a ^ b]] = -1.0 if (swaps & 1) else 1.0
    grades = np.array([bin(m).count("1") for m in masks])
    rev = np.where((grades * (grades - 1) // 2) % 2 == 1, -1.0, 1.0)
    return C, rev


def host_constants():
    C, rev = _ga_tables()
    biv = C[:, :, [4, 5, 6]]
    bk3 = np.ascontiguousarray(biv.transpose(2, 0, 1).reshape(3, 64)).astype(np.float32)
    U = np.zeros((64, 128), np.float32)
    V = np.zeros((64, 128), np.float32)
    for i in range(8):
        for j in range(8):
            for d in range(16):
                U[i * 8 + j, d * 8 + i] = 1.0
                V[i * 8 + j, d * 8 + j] = 1.0
    M128 = np.zeros((128, 128), np.float32)
    for d in range(16):
        M128[d * 8:(d + 1) * 8, d * 8:(d + 1) * 8] = 1.0
    revcol = np.zeros((64, 1), np.float32)
    ecol = np.zeros((64, 1), np.float32)
    for i in range(8):
        for j in range(8):
            revcol[i * 8 + j, 0] = rev[i]
            if i == j:
                ecol[i * 8 + j, 0] = SCALE * rev[j]
    return dict(c_bk3=bk3, c_u=U, c_v=V, c_m=M128, c_rev=revcol, c_e=ecol,
                c_id=np.eye(128, dtype=np.float32),
                c_ones=np.ones((1, 128), np.float32))


class Plan:
    def __init__(self):
        self.ops = {e: [] for e in ENGINES}
        self.cnt = {}

    def inc(self, sem, n=1):
        c = self.cnt.get(sem.name, 0) + n
        self.cnt[sem.name] = c
        return c

    def val(self, sem):
        return self.cnt.get(sem.name, 0)

    def emit(self, eng, fn):
        self.ops[eng].append(fn)

    def wait(self, eng, sem, v):
        if v is None or v <= 0:
            return
        self.emit(eng, lambda e, s=sem, v=v: e.wait_ge(s, v))

    def wait_full(self, eng, sem):
        self.wait(eng, sem, self.val(sem))

    def op(self, eng, build, sem=None, n=1):
        if sem is None:
            self.emit(eng, build)
            return None
        v = self.inc(sem, n)
        self.emit(eng, lambda e, b=build, s=sem, n=n: b(e).then_inc(s, n))
        return v

    def dma(self, eng, out, in_, sem, transpose=False):
        v = self.inc(sem, 16)
        if transpose:
            self.emit(eng, lambda e, o=out, i=in_, s=sem:
                      e.dma_start(out=o, in_=i, transpose=True).then_inc(s, 16))
        else:
            self.emit(eng, lambda e, o=out, i=in_, s=sem:
                      e.dma_start(out=o, in_=i).then_inc(s, 16))
        return v


class Stream:

    def __init__(self, pl, sems):
        self.pl, self.sems, self.n = pl, sems, len(sems)

    def dma(self, eng, out, in_, idx, transpose=False):
        return self.pl.dma(eng, out, in_, self.sems[idx % self.n], transpose)

    def wait_done(self, eng, idx):
        s = self.sems[idx % self.n]
        self.pl.wait(eng, s, self.pl.val(s))

    def wait_all(self, eng):
        for s in self.sems:
            self.pl.wait(eng, s, self.pl.val(s))


def build_program(dbg=False):
    nc = bass.Bass("TRN2", target_bir_lowering=False)

    xb = nc.dram_tensor("xb", [BT, DM], F32, kind="ExternalInput").ap()
    wqT = nc.dram_tensor("wqT", [8, 128, 32 * 512], F32, kind="ExternalInput").ap()
    wkT = nc.dram_tensor("wkT", [8, 128, 32 * 512], F32, kind="ExternalInput").ap()
    wvT = nc.dram_tensor("wvT", [8, 128, 32 * 512], F32, kind="ExternalInput").ap()
    bq2 = nc.dram_tensor("bq2", [128, 32], F32, kind="ExternalInput").ap()
    bk2 = nc.dram_tensor("bk2", [128, 32], F32, kind="ExternalInput").ap()
    bv1 = nc.dram_tensor("bv1", [1, DM], F32, kind="ExternalInput").ap()
    bwT = nc.dram_tensor("bwT", [3, 16], F32, kind="ExternalInput").ap()
    gq = nc.dram_tensor("gq", [II, DM], I16, kind="ExternalInput").ap()
    uq = nc.dram_tensor("uq", [II, DM], I16, kind="ExternalInput").ap()
    dq = nc.dram_tensor("dq", [DM, II], I16, kind="ExternalInput").ap()
    gz = nc.dram_tensor("gz", [II, NG_GU], F32, kind="ExternalInput").ap()
    gs = nc.dram_tensor("gs", [II, NG_GU], F32, kind="ExternalInput").ap()
    uz = nc.dram_tensor("uz", [II, NG_GU], F32, kind="ExternalInput").ap()
    us = nc.dram_tensor("us", [II, NG_GU], F32, kind="ExternalInput").ap()
    dz = nc.dram_tensor("dz", [DM, NG_D], F32, kind="ExternalInput").ap()
    ds = nc.dram_tensor("ds", [DM, NG_D], F32, kind="ExternalInput").ap()
    cons = {k: nc.dram_tensor(k, list(v.shape), F32, kind="ExternalInput").ap()
            for k, v in host_constants().items()}
    out = nc.dram_tensor("out", [QT, DM], F32, kind="ExternalOutput").ap()

    d_ln1 = nc.dram_tensor("d_ln1", [8, 128, 32 * 512], BF16).ap()
    d_k = nc.dram_tensor("d_k", [DM, BT], BF16).ap()
    d_v = nc.dram_tensor("d_v", [BT, DM], BF16).ap()
    d_attn = nc.dram_tensor("d_attn", [QT, DM], BF16).ap()
    d_x1 = nc.dram_tensor("d_x1", [QT, DM], F32).ap()
    d_wg = nc.dram_tensor("d_wg", [II, DM], BF16).ap()
    d_wu = nc.dram_tensor("d_wu", [II, DM], BF16).ap()
    d_wd = nc.dram_tensor("d_wd", [DM, II], BF16).ap()
    d_h = nc.dram_tensor("d_h", [II, QT], BF16).ap()
    d_qmdump = nc.dram_tensor("d_qmdump", [128, 32 * 1024], BF16).ap()
    d_ln2dump = nc.dram_tensor("d_ln2dump", [128, 32 * 1024], BF16).ap()
    dbg_out = {}
    if dbg:
        for nm, t in (("dbg_ln1", d_ln1), ("dbg_k", d_k), ("dbg_v", d_v),
                      ("dbg_attn", d_attn), ("dbg_x1", d_x1), ("dbg_h", d_h),
                      ("dbg_wg", d_wg), ("dbg_wd", d_wd)):
            dt = F32 if nm == "dbg_x1" else BF16
            dbg_out[nm] = nc.dram_tensor(nm, list(t.tensor.shape), dt,
                                         kind="ExternalOutput").ap()
        for nm in ("dbg_qm", "dbg_ln2"):
            dbg_out[nm] = nc.dram_tensor(nm, [128, 32 * 1024], BF16,
                                         kind="ExternalOutput").ap()
        dbg_out["dbg_bh"] = nc.dram_tensor("dbg_bh", [128, NH * 128], BF16,
                                           kind="ExternalOutput").ap()
        dbg_out["dbg_wkt"] = nc.dram_tensor("dbg_wkt", [64, 16], F32,
                                            kind="ExternalOutput").ap()

    with ExitStack() as ctx:
        sb = lambda n, s, d: ctx.enter_context(nc.sbuf_tensor(n, s, d))
        ps = lambda n, s, d: ctx.enter_context(nc.psum_tensor(n, s, d))
        sem = lambda n: ctx.enter_context(nc.semaphore(n))

        s_bq = sb("s_bq", [128, 32], F32)
        s_bk = sb("s_bk", [128, 32], F32)
        s_bv = sb("s_bv", [1, DM], BF16)
        s_ones = sb("s_ones", [1, 128], BF16)
        s_idb = sb("s_idb", [128, 128], BF16)
        s_idf = sb("s_idf", [128, 128], F32)
        s_u = sb("s_u", [64, 128], F32)
        s_uw = sb("s_uw", [64, 128], F32)
        s_v = sb("s_v", [64, 128], F32)
        s_m = sb("s_m", [128, 128], F32)
        s_rev = sb("s_rev", [64, 1], F32)
        s_e = sb("s_e", [64, 1], F32)
        s_bwT = sb("s_bwT", [3, 16], F32)
        s_bk3 = sb("s_bk3", [3, 64], F32)
        s_wk16 = sb("s_wk16", [16, 64], F32)
        s_wkT = sb("s_wkT", [64, 16], F32)
        s_bh = sb("s_bh", [128, NH * 128], BF16)
        s_col = sb("s_col", [128, 16], F32)
        s_den = sb("s_den", [128, 16], F32)
        s_mx8 = sb("s_mx8", [128, 16], F32)
        s_dsum = sb("s_dsum", [128, 2], F32)
        s_rcp = sb("s_rcp", [128, 1], F32)
        s_ngm = sb("s_ngm", [128, 2], F32)
        s_ev = sb("s_ev", [128, 2 * 512], BF16)
        s_dqi = sb("s_dqi", [128, 2 * 2048], I16)
        s_dqo = sb("s_dqo", [128, 2 * 2048], BF16)
        s_zs = sb("s_zs", [128, 64], F32)
        s_qm = sb("s_qm", [128, 32 * 1024], BF16)
        A = sb("arena", [128, 49152], BF16)

        def cv(lo, hi, dt=BF16):
            v = A[:, lo:hi]
            return v if dt is BF16 else v.bitcast(dt)

        a_str = (cv(0, 16384), cv(16384, 32768))
        a_xt1s = (cv(16384, 24576, F32), cv(24576, 32768, F32))
        a_lt1s = (cv(32768, 36864), cv(36864, 40960))
        a_sq1s = (cv(40960, 45056), cv(45056, 49152))
        a_w8 = cv(32768, 49152)
        a_kh = cv(0, 16384)
        a_vh = cv(16384, 24576)
        a_lg2 = (cv(24576, 32768, F32), cv(32768, 40960, F32))
        a_atf = cv(40960, 45056)
        a_atT = cv(45056, 46080)
        a_ao = cv(46080, 46592)
        a_xt4 = cv(0, 8192, F32)
        a_lt4 = cv(8192, 12288)
        a_lt4b = cv(12288, 16384)
        a_sq4 = cv(16384, 20480)
        a_x14 = cv(20480, 28672, F32)
        a_wt = (cv(0, 16384), cv(16384, 32768))
        a_gate = cv(32768, 36864)
        a_h5 = cv(36864, 38912)
        a_h2 = cv(0, 4096)
        a_rhs = cv(4096, 6144)
        a_lt7 = cv(6144, 10240)
        a_sq7 = cv(10240, 14336)
        a_x17 = cv(14336, 22528, F32)
        a_ot7 = cv(22528, 30720, F32)

        p0 = ps("p0", [128, 512], F32)
        p1 = ps("p1", [128, 512], F32)
        p2 = ps("p2", [128, 512], F32)
        p3 = ps("p3", [128, 512], F32)
        p4 = ps("p4", [128, 512], F32)
        p5 = ps("p5", [128, 512], F32)
        pt0 = ps("pt0", [128, 512], F32)
        pt1 = ps("pt1", [128, 512], F32)
        vt0 = pt0[:].bitcast(BF16)
        vt1 = pt1[:].bitcast(BF16)

        PE = sem("pe")
        DV = sem("dv")
        AC = sem("ac")
        PO = sem("po")
        CON = sem("con")
        WQL = sem("wql")
        STRL = [sem("strl0"), sem("strl1")]
        STRS = [sem("strs0"), sem("strs1")]
        KVS = [sem("kvs0"), sem("kvs1")]
        KHL = sem("khl")
        VHL = sem("vhl")
        AOS = [sem("aos0"), sem("aos1")]
        XAL = sem("xal")
        X1S = [sem("x1s0"), sem("x1s1")]
        DQL = sem("dql")
        DQZ = sem("dqz")
        DQS = [sem("dqs0"), sem("dqs1")]
        WTL = [sem("wtl0"), sem("wtl1")]
        HS = [sem("hs0"), sem("hs1")]
        HL = [sem(f"hl{i}") for i in range(4)]
        RL = [sem(f"rl{i}") for i in range(4)]
        X1L = sem("x1l")
        OS = sem("os")

        pl = Plan()
        st_strip = Stream(pl, STRS)
        ld_strip = Stream(pl, STRL)
        st_kv = Stream(pl, KVS)
        st_ao = Stream(pl, AOS)
        st_x1 = Stream(pl, X1S)
        st_dq = Stream(pl, DQS)
        ld_wt = Stream(pl, WTL)
        st_h = Stream(pl, HS)
        ld_h = Stream(pl, HL)
        ld_r = Stream(pl, RL)

        slabs = []
        for r0 in range(0, II, 128):
            for g0 in range(0, NG_GU, 16):
                slabs.append((gq, gz, gs, d_wg, r0, g0, 16))
                slabs.append((uq, uz, us, d_wu, r0, g0, 16))
        for r0 in range(0, DM, 128):
            for g0 in range(0, NG_D, 16):
                ng = min(16, NG_D - g0)
                slabs.append((dq, dz, ds, d_wd, r0, g0, ng))
        n_gu_slabs = II // 128 * 4
        deq_state = {"i": 0, "po0": 0, "po1": 0}
        deq_store_val = {}

        def pump_deq(n):
            for _ in range(n):
                idx = deq_state["i"]
                if idx >= len(slabs):
                    return
                deq_state["i"] = idx + 1
                q_in, z_in, s_in, w_out, r0, g0, ng = slabs[idx]
                b = idx % 2
                ncols = ng * GRP
                c0 = g0 * GRP
                dqi = s_dqi[:, b * 2048: b * 2048 + ncols]
                dqo = s_dqo[:, b * 2048: b * 2048 + ncols]
                zs = s_zs[:, b * 32: b * 32 + 2 * ng]
                pl.wait("sync", PO, deq_state[f"po{b}"])
                v_q = pl.dma("sync", dqi, q_in[r0:r0 + 128, c0:c0 + ncols], DQL)
                pl.dma("sync", zs[:, 0:ng], z_in[r0:r0 + 128, g0:g0 + ng], DQZ)
                v_z = pl.dma("sync", zs[:, ng:2 * ng], s_in[r0:r0 + 128, g0:g0 + ng],
                             DQZ)
                pl.wait("vector", DQL, v_q)
                pl.wait("vector", DQZ, v_z)
                st_dq.wait_done("vector", idx)
                for g in range(ng):
                    pl.op("vector", lambda e, dqo=dqo, dqi=dqi, zs=zs, g=g, ng=ng:
                          e.tensor_scalar(dqo[:, g * GRP:(g + 1) * GRP],
                                          dqi[:, g * GRP:(g + 1) * GRP],
                                          zs[:, g:g + 1], zs[:, ng + g:ng + g + 1],
                                          ALU.subtract, ALU.mult),
                          PO if g == ng - 1 else None)
                deq_state[f"po{b}"] = pl.val(PO)
                pl.wait("gpsimd", PO, deq_state[f"po{b}"])
                st_dq.dma("gpsimd", w_out[r0:r0 + 128, c0:c0 + ncols], dqo, idx)
                deq_store_val[idx] = (DQS[idx % 2], pl.val(DQS[idx % 2]))

        for dst, src in ((s_bq, bq2), (s_bk, bk2), (s_u, cons["c_u"]),
                         (s_v, cons["c_v"]), (s_m, cons["c_m"]),
                         (s_rev, cons["c_rev"]), (s_e, cons["c_e"]),
                         (s_idf, cons["c_id"]), (s_bwT, bwT), (s_bk3, cons["c_bk3"])):
            pl.dma("sync", dst[:], src, CON)
        v_con = pl.val(CON)
        pl.dma("gpsimd", s_idb[:], cons["c_id"], WQL)
        pl.dma("gpsimd", s_ones[:], cons["c_ones"], WQL)
        v_cast = pl.dma("gpsimd", s_bv[:], bv1, WQL)

        pl.wait("tensor", CON, v_con)
        pl.wait("vector", CON, v_con)
        v = pl.op("tensor", lambda e: e.matmul(p0[:16, :64], s_bwT[:], s_bk3[:],
                                               start=True, stop=True), PE)
        pl.wait("vector", PE, v)
        v = pl.op("vector", lambda e: e.tensor_copy(s_wk16[:], p0[:16, :64]), DV)
        pl.wait("tensor", DV, v)
        v = pl.op("tensor", lambda e: e.transpose(p1[:64, :16], s_wk16[:],
                                                  s_idf[:16, :16]), PE)
        pl.wait("vector", PE, v)
        v = pl.op("vector", lambda e: e.tensor_scalar(
            s_wkT[:], p1[:64, :16], s_rev[:, 0:1], s_e[:, 0:1], ALU.mult, ALU.add), DV)
        pl.wait("tensor", DV, v)
        pl.wait("vector", DV, v)
        pl.wait("tensor", WQL, v_cast)
        v_mm_bh = 0
        for h in range(NH):
            pl.wait("vector", PE, v_mm_bh)
            v = pl.op("vector", lambda e, h=h: e.tensor_scalar(
                s_uw[:], s_u[:], s_wkT[:, h:h + 1], None, ALU.mult), DV)
            pl.wait("tensor", DV, v)
            v_mm_bh = pl.op("tensor", lambda e: e.matmul(
                p2[:, :128], s_uw[:], s_v[:], start=True, stop=True), PE)
            pl.wait("vector", PE, v_mm_bh)
            v = pl.op("vector", lambda e, h=h: e.tensor_mul(
                s_bh[:, h * 128:(h + 1) * 128], p2[:, :128], s_m[:]), DV)
        v_bh_done = pl.val(DV)

        def plan_ln_tile(src_ap, out_ap, sqs, base=0):
            cS, cMu, cE2, cNV, cRS, cNM = (s_col[:, base + i:base + i + 1]
                                           for i in range(6))
            v1 = pl.op("vector", lambda e: e.tensor_reduce(cS, src_ap, AX.X, ALU.add),
                       DV)
            v2 = pl.op("scalar", lambda e: e.activation(sqs, src_ap, AF.Square,
                                                        accum_out=cE2), AC)
            pl.wait("vector", DV, v1)
            pl.wait("vector", AC, v2)
            v = pl.op("vector", lambda e: e.tensor_scalar(cMu, cS, 1.0 / DM, None,
                                                          ALU.mult), DV)
            pl.wait("vector", DV, v)
            v = pl.op("vector", lambda e: e.tensor_scalar(cE2, cE2, 1.0 / DM, None,
                                                          ALU.mult), DV)
            pl.wait("vector", DV, v)
            v = pl.op("vector", lambda e: e.scalar_tensor_tensor(
                cNV, cMu, cMu, cE2, ALU.mult, ALU.subtract), DV)
            pl.wait("scalar", DV, v)
            v = pl.op("scalar", lambda e: e.activation(cRS, cNV, AF.Sqrt,
                                                       bias=np.float32(EPS), scale=np.float32(-1.0)),
                      AC)
            pl.wait("vector", AC, v)
            v = pl.op("vector", lambda e: e.reciprocal(cRS, cRS), DV)
            pl.wait("vector", DV, v)
            v = pl.op("vector", lambda e: e.scalar_tensor_tensor(
                cNM, cMu, -1.0, cRS, ALU.mult, ALU.mult), DV)
            pl.wait("scalar", DV, v)
            return pl.op("scalar", lambda e: e.activation(
                out_ap, src_ap, AF.Identity, bias=cNM, scale=cRS), AC)

        last_evac = {"pt0": 0, "pt1": 0}

        def plan_tok_to_fm(tok_ap, v_ready_ac, dst_fn):
            pl.wait("tensor", AC, v_ready_ac)
            for cg in range(8):
                bname = ("pt0", "pt1")[cg % 2]
                bank = (vt0, vt1)[cg % 2]
                pl.wait("tensor", DV, last_evac[bname])
                for j in range(4):
                    cc = cg * 4 + j
                    pl.op("tensor", lambda e, bank=bank, j=j, cc=cc, tok=tok_ap:
                          e.transpose(bank[:, j * 128:(j + 1) * 128],
                                      tok[:, cc * 128:(cc + 1) * 128], s_idb[:]),
                          PE if j == 3 else None)
                v_pe = pl.val(PE)
                pl.wait("vector", PE, v_pe)
                last_evac[bname] = pl.op(
                    "vector", lambda e, cg=cg, bank=bank: dst_fn(e, cg, bank[:, 0:512]),
                    DV)
            return pl.val(DV)

        slot_apply = [None, None]
        slot_pe = [0, 0]
        for strip in range(8):
            sbh = a_str[0]
            st_strip.wait_all("vector")
            for tt in range(4):
                t0 = strip * 512 + tt * 128
                s = (strip * 4 + tt) % 2
                if slot_apply[s] is not None:
                    pl.wait("sync", AC, slot_apply[s])
                v_x = pl.dma("sync", a_xt1s[s], xb[t0:t0 + 128, :], XAL)
                pl.wait("vector", XAL, v_x)
                pl.wait("scalar", XAL, v_x)
                pl.wait("scalar", PE, slot_pe[s])
                slot_apply[s] = plan_ln_tile(a_xt1s[s], a_lt1s[s], a_sq1s[s],
                                             base=s * 8)

                def dst_strip(e, cg, bank, tt=tt, sbh=sbh):
                    dst = sbh.rearrange("p (cc q) -> p cc q", q=512)[
                        :, cg * 4:(cg + 1) * 4, tt * 128:(tt + 1) * 128]
                    src = bank.rearrange("p (j q) -> p j q", q=128)
                    return e.tensor_copy(dst, src)
                plan_tok_to_fm(a_lt1s[s], slot_apply[s], dst_strip)
                slot_pe[s] = pl.val(PE)
            pl.wait("gpsimd", DV, pl.val(DV))
            st_strip.dma("gpsimd", d_ln1[strip], sbh, strip)
            pump_deq(3)

        def load_w_eighth(wT_src, e8):
            pl.wait("gpsimd", PE, pl.val(PE))
            vs = []
            for i in range(4):
                vs.append(pl.dma("gpsimd", a_w8[:, i * 4096:(i + 1) * 4096],
                                 wT_src[e8][:, i * 4096:(i + 1) * 4096], WQL))
            return vs

        strip_buf_pe = [0, 0]

        def load_ln1_strip(strip):
            bi = strip % 2
            pl.wait("sync", PE, strip_buf_pe[bi])
            ld_strip.dma("sync", a_str[bi][:], d_ln1[strip], strip)
            return a_str[bi], bi

        st_strip.wait_all("sync")
        pl.wait("scalar", CON, v_con)
        pl.wait("gpsimd", AC, pl.val(AC))
        pl.wait("gpsimd", DV, pl.val(DV))
        pl.wait("sync", AC, pl.val(AC))
        pl.wait("sync", DV, pl.val(DV))

        kv_cnt = 0
        for e8 in range(8):
            v_ws = load_w_eighth(wkT, e8)
            for strip in range(8):
                lbuf, bi = load_ln1_strip(strip)
                pl.wait("tensor", STRL[bi], pl.val(STRL[bi]))
                for oc in range(4):
                    og = e8 * 4 + oc
                    bank = (p0, p1, p2, p3)[oc % 4]
                    pl.wait("tensor", AC, max(0, pl.val(AC) - 3))
                    for cc in range(32):
                        if strip == 0 and oc == 0 and cc % 8 == 0:
                            pl.wait("tensor", WQL, v_ws[cc // 8])
                        pl.op("tensor", lambda e, bank=bank, cc=cc, oc=oc, lbuf=lbuf:
                              e.matmul(bank[:],
                                       a_w8[:, cc * 512 + oc * 128:cc * 512 + oc * 128 + 128],
                                       lbuf[:, cc * 512:(cc + 1) * 512],
                                       start=(cc == 0), stop=(cc == 31)),
                              PE if cc == 31 else None)
                    v_pe = pl.val(PE)
                    eb = s_ev[:, (kv_cnt % 2) * 512:(kv_cnt % 2) * 512 + 512]
                    pl.wait("scalar", PE, v_pe)
                    st_kv.wait_done("scalar", kv_cnt)
                    v_ac = pl.op("scalar", lambda e, bank=bank, eb=eb, og=og:
                                 e.activation(eb, bank[:], AF.Identity,
                                              bias=s_bk[:, og:og + 1], scale=1.0), AC)
                    pl.wait("gpsimd", AC, v_ac)
                    st_kv.dma("gpsimd", d_k[og * 128:(og + 1) * 128,
                                            strip * 512:(strip + 1) * 512], eb, kv_cnt)
                    kv_cnt += 1
                strip_buf_pe[bi] = pl.val(PE)
                pump_deq(2)

        for e8 in range(8):
            v_ws = load_w_eighth(wvT, e8)
            o0 = e8 * 512
            for strip in range(8):
                lbuf, bi = load_ln1_strip(strip)
                pl.wait("tensor", STRL[bi], pl.val(STRL[bi]))
                for ts in range(4):
                    t0 = strip * 512 + ts * 128
                    bank = (p0, p1, p2, p3)[ts % 4]
                    pl.wait("tensor", DV, max(0, pl.val(DV) - 3))
                    for cc in range(32):
                        if strip == 0 and ts == 0 and cc % 8 == 0:
                            pl.wait("tensor", WQL, v_ws[cc // 8])
                        pl.op("tensor", lambda e, bank=bank, cc=cc, ts=ts, lbuf=lbuf:
                              e.matmul(bank[:],
                                       lbuf[:, cc * 512 + ts * 128:cc * 512 + ts * 128 + 128],
                                       a_w8[:, cc * 512:(cc + 1) * 512],
                                       start=(cc == 0), stop=False), None)
                    pl.op("tensor", lambda e, bank=bank, o0=o0:
                          e.matmul(bank[:], s_ones[0:1, :], s_bv[0:1, o0:o0 + 512],
                                   start=False, stop=True), PE)
                    v_pe = pl.val(PE)
                    eb = s_ev[:, (kv_cnt % 2) * 512:(kv_cnt % 2) * 512 + 512]
                    pl.wait("vector", PE, v_pe)
                    st_kv.wait_done("vector", kv_cnt)
                    v_dv = pl.op("vector", lambda e, bank=bank, eb=eb:
                                 e.tensor_copy(eb, bank[:]), DV)
                    pl.wait("gpsimd", DV, v_dv)
                    st_kv.dma("gpsimd", d_v[t0:t0 + 128, o0:o0 + 512], eb, kv_cnt)
                    kv_cnt += 1
                strip_buf_pe[bi] = pl.val(PE)
                pump_deq(2)

        pl.wait("tensor", DV, v_bh_done)
        q_eb_pe = [0, 0]
        for e8 in range(8):
            v_ws = load_w_eighth(wqT, e8)
            for strip in range(2):
                lbuf, bi = load_ln1_strip(strip)
                pl.wait("tensor", STRL[bi], pl.val(STRL[bi]))
                for oc in range(4):
                    og = e8 * 4 + oc
                    head = og // 2
                    bank = (p0, p1)[oc % 2]
                    pl.wait("tensor", AC, max(0, pl.val(AC) - 1))
                    for cc in range(32):
                        if strip == 0 and oc == 0 and cc % 8 == 0:
                            pl.wait("tensor", WQL, v_ws[cc // 8])
                        pl.op("tensor", lambda e, bank=bank, cc=cc, oc=oc, lbuf=lbuf:
                              e.matmul(bank[:],
                                       a_w8[:, cc * 512 + oc * 128:cc * 512 + oc * 128 + 128],
                                       lbuf[:, cc * 512:(cc + 1) * 512],
                                       start=(cc == 0), stop=(cc == 31)),
                              PE if cc == 31 else None)
                    v_pe = pl.val(PE)
                    eb = s_ev[:, (oc % 2) * 512:(oc % 2) * 512 + 512]
                    pl.wait("scalar", PE, v_pe)
                    pl.wait("scalar", PE, q_eb_pe[oc % 2])
                    v_ac = pl.op("scalar", lambda e, bank=bank, eb=eb, og=og:
                                 e.activation(eb, bank[:], AF.Identity,
                                              bias=s_bq[:, og:og + 1], scale=1.0), AC)
                    pl.wait("tensor", AC, v_ac)
                    v_tr = pl.op("tensor", lambda e, eb=eb, head=head:
                                 e.matmul(p4[:], s_bh[:, head * 128:(head + 1) * 128],
                                          eb, start=True, stop=True), PE)
                    q_eb_pe[oc % 2] = v_tr
                    pl.wait("vector", PE, v_tr)
                    pl.op("vector", lambda e, og=og, strip=strip:
                          e.tensor_copy(s_qm[:, og * 1024 + strip * 512:
                                             og * 1024 + (strip + 1) * 512], p4[:]),
                          DV)
                strip_buf_pe[bi] = pl.val(PE)
            pump_deq(2)
        v_qmod_done = pl.val(DV)
        if dbg:
            pl.wait("sync", DV, v_qmod_done)
            pl.dma("sync", d_qmdump, s_qm[:], X1L)

        st_kv.wait_all("sync")
        pl.wait("tensor", DV, v_qmod_done)
        v_pe_p3 = pl.val(PE)
        kh_pe = [v_pe_p3, v_pe_p3]
        vh_pe = [v_pe_p3]
        ao_cnt = 0
        v_atf_pe = 0
        v_dsum_prev = [0, 0]
        lg_busy_ac = [0, 0]
        lgevac = {}
        v_pv_evac = 0
        v_ngm_prev = [0, 0]
        kv_loads = {}

        def emit_k_load(h):
            kb = h % 2
            khbuf = a_kh[:, kb * (2 * BT):(kb + 1) * (2 * BT)]
            pl.wait("sync", PE, kh_pe[kb])
            v_k = None
            for fc in range(2):
                v_k = pl.dma("sync", khbuf[:, fc * BT:(fc + 1) * BT],
                             d_k[h * 256 + fc * 128:h * 256 + (fc + 1) * 128, :], KHL)
            kv_loads[h] = (v_k, khbuf)

        def emit_v_load(h):
            pl.wait("sync", PE, vh_pe[0])
            srcv = d_v.rearrange("(ms p) f -> p ms f", p=128)[
                :, :, h * 256:(h + 1) * 256]
            dstv = a_vh.rearrange("p (ms f) -> p ms f", f=256)
            return pl.dma("sync", dstv, srcv, VHL)

        def pass1(hq):
            h, qt = divmod(hq, 8)
            lq = qt % 2
            lg = a_lg2[lq]
            v_k, khbuf = kv_loads[h]
            if qt == 0:
                pl.wait("tensor", KHL, v_k)
            for mt in range(8):
                bank = (p0, p1, p2, p3)[mt % 4]
                pl.wait("tensor", AC, lgevac.get(id(bank), 0))
                for fc in range(2):
                    pl.op("tensor",
                          lambda e, bank=bank, fc=fc, qt=qt, h=h, mt=mt, khbuf=khbuf:
                          e.matmul(bank[:],
                                   s_qm[:, (h * 2 + fc) * 1024 + qt * 128:
                                        (h * 2 + fc) * 1024 + qt * 128 + 128],
                                   khbuf[:, fc * BT + mt * 512:
                                         fc * BT + (mt + 1) * 512],
                                   start=(fc == 0), stop=(fc == 1)),
                          PE if fc == 1 else None)
                v_pe = pl.val(PE)
                pl.wait("scalar", PE, v_pe)
                if mt == 0:
                    pl.wait("scalar", AC, lg_busy_ac[lq])
                v_ev = pl.op("scalar", lambda e, bank=bank, mt=mt, lg=lg:
                             e.copy(lg[:, mt * 512:(mt + 1) * 512], bank[:]), AC)
                lgevac[id(bank)] = v_ev
                pl.wait("vector", AC, v_ev)
                if mt == 0:
                    pl.wait("vector", DV, v_ngm_prev[lq])
                pl.op("vector", lambda e, mt=mt, lq=lq, lg=lg: e.tensor_reduce(
                    s_mx8[:, lq * 8 + mt:lq * 8 + mt + 1],
                    lg[:, mt * 512:(mt + 1) * 512], AX.X, ALU.max, negate=True), DV)
            pl.wait("vector", DV, pl.val(DV))
            v_ngm = pl.op("vector", lambda e, lq=lq: e.tensor_reduce(
                s_ngm[:, lq:lq + 1], s_mx8[:, lq * 8:lq * 8 + 8], AX.X, ALU.min), DV)
            v_ngm_prev[lq] = v_ngm
            if qt == 7:
                kh_pe[h % 2] = pl.val(PE)
            return v_ngm

        def pass2(hq, v_ngm):
            nonlocal ao_cnt, v_atf_pe, v_pv_evac
            h, qt = divmod(hq, 8)
            lq = qt % 2
            lg = a_lg2[lq]
            if qt == 0:
                kv_loads[(h, "v")] = emit_v_load(h)
            v_v = kv_loads[(h, "v")]
            pl.wait("tensor", VHL, v_v)
            v_exps = [0] * 8
            trev_done = [0, 0]
            pv_done = [0, 0]
            tr_pe = [0] * 8

            def emit_exp(mt):
                if mt == 0:
                    pl.wait("scalar", DV, v_ngm)
                    pl.wait("scalar", DV, v_dsum_prev[lq])
                    pl.wait("scalar", PE, v_atf_pe)
                v_exps[mt] = pl.op(
                    "scalar", lambda e, lg=lg, lq=lq, mt=mt:
                    e.activation(a_atf[:, mt * 512:(mt + 1) * 512],
                                 lg[:, mt * 512:(mt + 1) * 512],
                                 AF.Exp, bias=s_ngm[:, lq:lq + 1], scale=1.0,
                                 accum_out=s_den[:, lq * 8 + mt:lq * 8 + mt + 1]),
                    AC)

            def emit_tr(mt):
                bank = (vt0, vt1)[mt % 2]
                bname = ("pt0", "pt1")[mt % 2]
                pl.wait("tensor", AC, v_exps[mt])
                pl.wait("tensor", DV, max(trev_done[mt % 2], last_evac[bname]))
                for j in range(4):
                    pl.op("tensor", lambda e, bank=bank, j=j, mt=mt:
                          e.transpose(bank[:, j * 128:(j + 1) * 128],
                                      a_atf[:, mt * 512 + j * 128:
                                            mt * 512 + (j + 1) * 128], s_idb[:]),
                          PE if j == 3 else None)
                tr_pe[mt] = pl.val(PE)
                pl.wait("vector", PE, tr_pe[mt])
                pl.wait("vector", PE, pv_done[mt % 2])
                tb = a_atT[:, (mt % 2) * 512:(mt % 2) * 512 + 512]
                v_dv = pl.op("vector", lambda e, tb=tb, bank=bank:
                             e.tensor_copy(tb, bank[:, 0:512]), DV)
                trev_done[mt % 2] = v_dv
                last_evac[bname] = v_dv
                return v_dv

            def emit_pv(mt, v_evac):
                tb = a_atT[:, (mt % 2) * 512:(mt % 2) * 512 + 512]
                pl.wait("tensor", DV, v_evac)
                if mt == 0:
                    pl.wait("tensor", DV, v_pv_evac)
                for j in range(4):
                    pl.op("tensor", lambda e, tb=tb, j=j, mt=mt:
                          e.matmul(p4[:, 0:256], tb[:, j * 128:(j + 1) * 128],
                                   a_vh[:, (mt * 4 + j) * 256:
                                        (mt * 4 + j + 1) * 256],
                                   start=(mt == 0 and j == 0),
                                   stop=(mt == 7 and j == 3)),
                          PE if j == 3 else None)
                pv_done[mt % 2] = pl.val(PE)

            emit_exp(0)
            emit_exp(1)
            evs = [0] * 8
            for mt in range(8):
                if mt + 2 < 8:
                    emit_exp(mt + 2)
                evs[mt] = emit_tr(mt)
                if mt >= 1:
                    emit_pv(mt - 1, evs[mt - 1])
            emit_pv(7, evs[7])
            v_pv_all = pl.val(PE)
            v_atf_pe = v_pv_all
            if qt == 7:
                vh_pe[0] = v_pv_all
            pl.wait("vector", AC, v_exps[7])
            v_ds = pl.op("vector", lambda e, lq=lq: e.tensor_reduce(
                s_dsum[:, lq:lq + 1], s_den[:, lq * 8:lq * 8 + 8], AX.X, ALU.add), DV)
            v_dsum_prev[lq] = v_ds
            pl.wait("vector", DV, v_ds)
            v_rc = pl.op("vector", lambda e, lq=lq:
                         e.reciprocal(s_rcp[:], s_dsum[:, lq:lq + 1]), DV)
            pl.wait("vector", DV, v_rc)
            pl.wait("vector", PE, v_pv_all)
            aob = a_ao[:, (ao_cnt % 2) * HD:(ao_cnt % 2) * HD + HD]
            st_ao.wait_done("vector", ao_cnt)
            v_pv_evac = pl.op("vector", lambda e, aob=aob: e.tensor_scalar(
                aob, p4[:, 0:256], s_rcp[:, 0:1], None, ALU.mult), DV)
            pl.wait("scalar", DV, v_pv_evac)
            st_ao.dma("scalar", d_attn[qt * 128:(qt + 1) * 128,
                                       h * 256:(h + 1) * 256], aob, ao_cnt)
            ao_cnt += 1

        emit_k_load(0)
        ngms = {0: pass1(0)}
        for hq in range(NH * 8):
            h2 = (hq + 2) // 8
            if h2 < NH and h2 not in kv_loads:
                emit_k_load(h2)
            if hq + 1 < NH * 8:
                ngms[hq + 1] = pass1(hq + 1)
            pass2(hq, ngms[hq])
            pump_deq(2)

        st_ao.wait_all("sync")
        pl.wait("sync", AC, pl.val(AC))
        v_apply2 = None
        v_add1 = 0
        for tt in range(8):
            t0 = tt * 128
            pl.wait("sync", PE, pl.val(PE))
            if v_apply2 is not None:
                pl.wait("sync", AC, v_apply2)
                pl.wait("sync", DV, v_add1)
            pl.dma("sync", a_lt4, d_attn[t0:t0 + 128, :], XAL)
            v_x = pl.dma("sync", a_xt4, xb[t0:t0 + 128, :], XAL)
            pl.wait("vector", XAL, v_x)
            pl.wait("scalar", XAL, v_x)
            v_apply = plan_ln_tile(a_lt4, a_lt4b, a_sq4)
            pl.wait("vector", AC, v_apply)
            st_x1.wait_done("vector", tt)
            v_add1 = pl.op("vector", lambda e: e.tensor_add(a_x14, a_xt4, a_lt4b), DV)
            pl.wait("gpsimd", DV, v_add1)
            st_x1.dma("gpsimd", d_x1[t0:t0 + 128, :], a_x14, tt)
            pl.wait("vector", DV, v_add1)
            pl.wait("scalar", DV, v_add1)
            v_apply2 = plan_ln_tile(a_x14, a_lt4, a_sq4)

            def dst_qm(e, cg, bank, tt=tt):
                dst = s_qm.rearrange("p (cc q) -> p cc q", q=1024)[
                    :, cg * 4:(cg + 1) * 4, tt * 128:(tt + 1) * 128]
                src = bank.rearrange("p (j q) -> p j q", q=128)
                return e.tensor_copy(dst, src)
            plan_tok_to_fm(a_lt4, v_apply2, dst_qm)
            pump_deq(2)
        v_ln2_done = pl.val(DV)
        if dbg:
            pl.wait("sync", DV, v_ln2_done)
            pl.dma("sync", d_ln2dump, s_qm[:], X1L)

        def wait_deq_upto(idx):
            if deq_state["i"] <= idx:
                pump_deq(idx + 1 - deq_state["i"])
            for k in (idx, idx - 1):
                if k >= 0 and k in deq_store_val:
                    s, v = deq_store_val[k]
                    pl.wait("sync", s, v)

        pl.wait("tensor", DV, v_ln2_done)
        pl.wait("sync", AC, pl.val(AC))
        pl.wait("sync", DV, pl.val(DV))
        v_pe_p5 = pl.val(PE)
        wq_pe = [v_pe_p5, v_pe_p5]
        h_cnt = 0
        blocks = [list(range(NG_D))[i:i + 4] for i in range(0, NG_D, 4)]
        for bi2, blk in enumerate(blocks):
            nbo = len(blk) * 128
            for which, wsrc in ((0, d_wg), (1, d_wu)):
                wb = which
                wtile = a_wt[wb][:, 0:32 * nbo]
                wait_deq_upto(min((blk[-1] + 1) * 4, n_gu_slabs) - 1)
                pl.wait("sync", PE, wq_pe[wb])
                for cc in range(32):
                    src = wsrc[blk[0] * 128: blk[0] * 128 + nbo,
                               cc * 128:(cc + 1) * 128]
                    ld_wt.dma("sync", wtile[:, cc * nbo:(cc + 1) * nbo], src,
                              wb, transpose=True)
                pl.wait("tensor", WTL[wb], pl.val(WTL[wb]))
                if which == 0:
                    pl.wait("scalar", DV, pl.val(DV))
                for j, oc in enumerate(blk):
                    for t2 in range(2):
                        bank = (p0, p1, p2, p3)[(j * 2 + t2) % 4]
                        if which == 0:
                            pl.wait("tensor", AC, max(0, pl.val(AC) - 3))
                        else:
                            pl.wait("tensor", DV, max(0, pl.val(DV) - 3))
                        for cc in range(32):
                            pl.op("tensor",
                                  lambda e, bank=bank, cc=cc, j=j, t2=t2,
                                  wtile=wtile, nbo=nbo:
                                  e.matmul(bank[:],
                                           wtile[:, cc * nbo + j * 128:
                                                 cc * nbo + (j + 1) * 128],
                                           s_qm[:, cc * 1024 + t2 * 512:
                                                cc * 1024 + (t2 + 1) * 512],
                                           start=(cc == 0), stop=(cc == 31)),
                                  PE if cc == 31 else None)
                        v_pe = pl.val(PE)
                        if which == 0:
                            pl.wait("scalar", PE, v_pe)
                            pl.op("scalar", lambda e, bank=bank, j=j, t2=t2:
                                  e.activation(a_gate[:, j * 1024 + t2 * 512:
                                                      j * 1024 + (t2 + 1) * 512],
                                               bank[:], AF.Silu), AC)
                        else:
                            pl.wait("vector", PE, v_pe)
                            pl.wait("vector", AC, pl.val(AC))
                            st_h.wait_done("vector", h_cnt)
                            pl.op("vector",
                                  lambda e, bank=bank, j=j, t2=t2, hc=h_cnt:
                                  e.tensor_mul(a_h5[:, (hc % 2) * 1024 + t2 * 512:
                                                    (hc % 2) * 1024 + (t2 + 1) * 512],
                                               bank[:],
                                               a_gate[:, j * 1024 + t2 * 512:
                                                      j * 1024 + (t2 + 1) * 512]),
                                  DV)
                    if which == 1:
                        pl.wait("gpsimd", DV, pl.val(DV))
                        hb = a_h5[:, (h_cnt % 2) * 1024:(h_cnt % 2) * 1024 + 1024]
                        st_h.dma("gpsimd", d_h[oc * 128:(oc + 1) * 128, :], hb, h_cnt)
                        h_cnt += 1
                wq_pe[wb] = pl.val(PE)
            pump_deq(10)

        pump_deq(len(slabs))
        st_dq.wait_all("sync")
        st_h.wait_all("sync")
        banks8 = (p0, p1, p2, p3, p4, p5, pt0, pt1)
        v_pe_p6 = pl.val(PE)
        hl_pe = [v_pe_p6] * 4
        v_dv_p6 = pl.val(DV)
        for sl in range(8):
            for ic in range(NG_D):
                b = ic % 4
                hb = a_h2[:, b * 1024:b * 1024 + 1024]
                rb = a_rhs[:, b * 512:b * 512 + 512]
                pl.wait("sync", PE, hl_pe[b])
                ld_h.dma("sync", hb, d_h[ic * 128:(ic + 1) * 128, :], ic)
                ld_r.dma("sync", rb,
                         d_wd[sl * 512:(sl + 1) * 512, ic * 128:(ic + 1) * 128],
                         ic, transpose=True)
                pl.wait("tensor", HL[b], pl.val(HL[b]))
                pl.wait("tensor", RL[b], pl.val(RL[b]))
                if ic == 0:
                    pl.wait("tensor", DV, v_dv_p6)
                for tt in range(8):
                    pl.op("tensor", lambda e, tt=tt, hb=hb, rb=rb, ic=ic:
                          e.matmul(banks8[tt][:],
                                   hb[:, tt * 128:(tt + 1) * 128], rb,
                                   start=(ic == 0), stop=(ic == NG_D - 1)),
                          PE if tt == 7 else None)
                hl_pe[b] = pl.val(PE)
            v_pe = pl.val(PE)
            pl.wait("vector", PE, v_pe)
            for tt in range(8):
                pl.op("vector", lambda e, tt=tt, sl=sl:
                      e.tensor_copy(s_qm[:, tt * 4096 + sl * 512:
                                         tt * 4096 + (sl + 1) * 512],
                                    banks8[tt][:]), DV)
            v_dv_p6 = pl.val(DV)
        v_down_done = pl.val(DV)

        pl.wait("scalar", DV, v_down_done)
        pl.wait("vector", DV, v_down_done)
        pl.wait("scalar", PE, pl.val(PE))
        pl.wait("sync", PE, pl.val(PE))
        v_apply3 = None
        v_add_prev = 0
        n_out = 0
        for tt in range(8):
            t0 = tt * 128
            dtile = s_qm[:, tt * 4096:(tt + 1) * 4096]
            pl.wait("sync", DV, v_add_prev)
            v_x1 = pl.dma("sync", a_x17, d_x1[t0:t0 + 128, :], X1L)
            if v_apply3 is not None:
                pl.wait("vector", OS, 16 * tt)
                pl.wait("scalar", DV, v_add_prev)
            v_apply3 = plan_ln_tile(dtile, a_lt7, a_sq7)
            pl.wait("vector", AC, v_apply3)
            pl.wait("vector", X1L, v_x1)
            v_add_prev = pl.op("vector",
                               lambda e: e.tensor_add(a_ot7, a_x17, a_lt7), DV)
            pl.wait("sync", DV, v_add_prev)
            pl.dma("sync", out[t0:t0 + 128, :], a_ot7, OS)
            n_out += 1
        pl.wait("sync", OS, 16 * n_out)
        pl.wait("gpsimd", OS, 16 * n_out)

        if dbg:
            DGS = X1L
            for nm, src in (("dbg_ln1", d_ln1), ("dbg_k", d_k), ("dbg_v", d_v),
                            ("dbg_attn", d_attn), ("dbg_x1", d_x1), ("dbg_h", d_h),
                            ("dbg_wg", d_wg), ("dbg_wd", d_wd)):
                pl.dma("sync", dbg_out[nm], src, DGS)
            pl.dma("sync", dbg_out["dbg_qm"], d_qmdump, DGS)
            pl.dma("sync", dbg_out["dbg_ln2"], d_ln2dump, DGS)
            pl.dma("sync", dbg_out["dbg_bh"], s_bh[:], DGS)
            pl.dma("sync", dbg_out["dbg_wkt"], s_wkT[:], DGS)
            pl.wait("sync", DGS, pl.val(DGS))

        block = ctx.enter_context(nc.Block())
        for name in ENGINES:
            fns = pl.ops[name]
            getattr(block, name)(lambda e, fns=fns: [fn(e) for fn in fns])

    return nc


def make_in_maps(x, Wq, bq, Wk, bk, Wv, bv, bias_weight,
                 gqw, gsc, gze, uqw, usc, uze, dqw, dsc, dze):
    def c(a):
        return np.ascontiguousarray(a)

    def as_i16(q):
        return c(np.asarray(q, dtype=np.int32).view(np.int16)[:, ::2])

    def tile8(W):
        wt = np.asarray(W, np.float32).T.reshape(32, 128, 8, 512)
        return c(wt.transpose(2, 1, 0, 3).reshape(8, 128, 32 * 512))

    common = dict(
        wqT=tile8(Wq), wkT=tile8(Wk), wvT=tile8(Wv),
        bq2=c(np.asarray(bq, np.float32).reshape(32, 128).T),
        bk2=c(np.asarray(bk, np.float32).reshape(32, 128).T),
        bv1=c(np.asarray(bv, np.float32).reshape(1, DM)),
        bwT=c(np.asarray(bias_weight, np.float32).T),
        gq=as_i16(gqw), uq=as_i16(uqw), dq=as_i16(dqw),
        gz=c(np.asarray(gze, np.float32)), gs=c(np.asarray(gsc, np.float32)),
        uz=c(np.asarray(uze, np.float32)), us=c(np.asarray(usc, np.float32)),
        dz=c(np.asarray(dze, np.float32)), ds=c(np.asarray(dsc, np.float32)),
        **{k: c(v) for k, v in host_constants().items()})
    in_maps = []
    for core in range(8):
        b, q = core // 4, core % 4
        xb_b = np.asarray(x[b], np.float32)
        perm = np.concatenate([xb_b[q * QT:(q + 1) * QT],
                               xb_b[:q * QT], xb_b[(q + 1) * QT:]], axis=0)
        in_maps.append(dict(common, xb=c(perm)))
    return in_maps


def kernel(L0_attn_x, L0_attn_Wq, L0_attn_bq, L0_attn_Wk, L0_attn_bk,
           L0_attn_Wv, L0_attn_bv, L0_attn_bias_weight,
           L0_ffn_gate_qweight, L0_ffn_gate_scales, L0_ffn_gate_zeros,
           L0_ffn_up_qweight, L0_ffn_up_scales, L0_ffn_up_zeros,
           L0_ffn_down_qweight, L0_ffn_down_scales, L0_ffn_down_zeros):
    x = np.asarray(L0_attn_x, dtype=np.float32)
    assert x.shape == (2, BT, DM)
    in_maps = make_in_maps(x, L0_attn_Wq, L0_attn_bq, L0_attn_Wk, L0_attn_bk,
                           L0_attn_Wv, L0_attn_bv, L0_attn_bias_weight,
                           L0_ffn_gate_qweight, L0_ffn_gate_scales,
                           L0_ffn_gate_zeros, L0_ffn_up_qweight,
                           L0_ffn_up_scales, L0_ffn_up_zeros,
                           L0_ffn_down_qweight, L0_ffn_down_scales,
                           L0_ffn_down_zeros)
    nc = build_program()
    res = run_bass_kernel_spmd(nc, in_maps, core_ids=list(range(8)))
    outp = np.empty((2, BT, DM), np.float32)
    for core in range(8):
        b, q = core // 4, core % 4
        outp[b, q * QT:(q + 1) * QT] = res.results[core]["out"]
    return outp


# version 14
# speedup vs baseline: 1.0153x; 1.0153x over previous
import numpy as np
import ml_dtypes
import concourse.bass as bass
import concourse.mybir as mybir
from concourse.bass_utils import run_bass_kernel_spmd
from contextlib import ExitStack

F32 = mybir.dt.float32
BF16 = mybir.dt.bfloat16
I16 = mybir.dt.int16
AF = mybir.ActivationFunctionType
ALU = mybir.AluOpType
AX = mybir.AxisListType

ENGINES = ("sync", "tensor", "vector", "scalar", "gpsimd")

NB, CD, NH = 8, 32, 16
HD = CD * NB
DM = 4096
BT = 4096
QT = 1024
II = 11008
GRP = 128
NG_GU = DM // GRP
NG_D = II // GRP
SCALE = 1.0 / np.sqrt(HD)
EPS = 1e-5


def _ga_tables():
    n = 8
    masks = sorted(range(n), key=lambda m: (bin(m).count("1"), m))
    pos = {m: k for k, m in enumerate(masks)}
    C = np.zeros((n, n, n), dtype=np.float64)
    for ai, a in enumerate(masks):
        for bi, b in enumerate(masks):
            swaps, at = 0, a >> 1
            while at:
                swaps += bin(at & b).count("1")
                at >>= 1
            C[ai, bi, pos[a ^ b]] = -1.0 if (swaps & 1) else 1.0
    grades = np.array([bin(m).count("1") for m in masks])
    rev = np.where((grades * (grades - 1) // 2) % 2 == 1, -1.0, 1.0)
    return C, rev


def host_constants():
    C, rev = _ga_tables()
    biv = C[:, :, [4, 5, 6]]
    bk3 = np.ascontiguousarray(biv.transpose(2, 0, 1).reshape(3, 64)).astype(np.float32)
    U = np.zeros((64, 128), np.float32)
    V = np.zeros((64, 128), np.float32)
    for i in range(8):
        for j in range(8):
            for d in range(16):
                U[i * 8 + j, d * 8 + i] = 1.0
                V[i * 8 + j, d * 8 + j] = 1.0
    M128 = np.zeros((128, 128), np.float32)
    for d in range(16):
        M128[d * 8:(d + 1) * 8, d * 8:(d + 1) * 8] = 1.0
    revcol = np.zeros((64, 1), np.float32)
    ecol = np.zeros((64, 1), np.float32)
    for i in range(8):
        for j in range(8):
            revcol[i * 8 + j, 0] = rev[i]
            if i == j:
                ecol[i * 8 + j, 0] = SCALE * rev[j]
    return dict(c_bk3=bk3, c_u=U, c_v=V, c_m=M128, c_rev=revcol, c_e=ecol,
                c_id=np.eye(128, dtype=np.float32),
                c_ones=np.ones((1, 128), np.float32))


class Plan:
    def __init__(self):
        self.ops = {e: [] for e in ENGINES}
        self.cnt = {}

    def inc(self, sem, n=1):
        c = self.cnt.get(sem.name, 0) + n
        self.cnt[sem.name] = c
        return c

    def val(self, sem):
        return self.cnt.get(sem.name, 0)

    def emit(self, eng, fn):
        self.ops[eng].append(fn)

    def wait(self, eng, sem, v):
        if v is None or v <= 0:
            return
        self.emit(eng, lambda e, s=sem, v=v: e.wait_ge(s, v))

    def wait_full(self, eng, sem):
        self.wait(eng, sem, self.val(sem))

    def op(self, eng, build, sem=None, n=1):
        if sem is None:
            self.emit(eng, build)
            return None
        v = self.inc(sem, n)
        self.emit(eng, lambda e, b=build, s=sem, n=n: b(e).then_inc(s, n))
        return v

    def dma(self, eng, out, in_, sem, transpose=False):
        v = self.inc(sem, 16)
        if transpose:
            self.emit(eng, lambda e, o=out, i=in_, s=sem:
                      e.dma_start(out=o, in_=i, transpose=True).then_inc(s, 16))
        else:
            self.emit(eng, lambda e, o=out, i=in_, s=sem:
                      e.dma_start(out=o, in_=i).then_inc(s, 16))
        return v


class Stream:

    def __init__(self, pl, sems):
        self.pl, self.sems, self.n = pl, sems, len(sems)

    def dma(self, eng, out, in_, idx, transpose=False):
        return self.pl.dma(eng, out, in_, self.sems[idx % self.n], transpose)

    def wait_done(self, eng, idx):
        s = self.sems[idx % self.n]
        self.pl.wait(eng, s, self.pl.val(s))

    def wait_all(self, eng):
        for s in self.sems:
            self.pl.wait(eng, s, self.pl.val(s))


def build_program(dbg=False):
    nc = bass.Bass("TRN2", target_bir_lowering=False)

    xb = nc.dram_tensor("xb", [BT, DM], F32, kind="ExternalInput").ap()
    wqT = nc.dram_tensor("wqT", [8, 128, 32 * 512], F32, kind="ExternalInput").ap()
    wkT = nc.dram_tensor("wkT", [8, 128, 32 * 512], F32, kind="ExternalInput").ap()
    wvT = nc.dram_tensor("wvT", [8, 128, 32 * 512], F32, kind="ExternalInput").ap()
    bq2 = nc.dram_tensor("bq2", [128, 32], F32, kind="ExternalInput").ap()
    bk2 = nc.dram_tensor("bk2", [128, 32], F32, kind="ExternalInput").ap()
    bv1 = nc.dram_tensor("bv1", [1, DM], F32, kind="ExternalInput").ap()
    bwT = nc.dram_tensor("bwT", [3, 16], F32, kind="ExternalInput").ap()
    gq = nc.dram_tensor("gq", [II, DM], I16, kind="ExternalInput").ap()
    uq = nc.dram_tensor("uq", [II, DM], I16, kind="ExternalInput").ap()
    dq = nc.dram_tensor("dq", [DM, II], I16, kind="ExternalInput").ap()
    gz = nc.dram_tensor("gz", [II, NG_GU], F32, kind="ExternalInput").ap()
    gs = nc.dram_tensor("gs", [II, NG_GU], F32, kind="ExternalInput").ap()
    uz = nc.dram_tensor("uz", [II, NG_GU], F32, kind="ExternalInput").ap()
    us = nc.dram_tensor("us", [II, NG_GU], F32, kind="ExternalInput").ap()
    dz = nc.dram_tensor("dz", [DM, NG_D], F32, kind="ExternalInput").ap()
    ds = nc.dram_tensor("ds", [DM, NG_D], F32, kind="ExternalInput").ap()
    cons = {k: nc.dram_tensor(k, list(v.shape), F32, kind="ExternalInput").ap()
            for k, v in host_constants().items()}
    out = nc.dram_tensor("out", [QT, DM], F32, kind="ExternalOutput").ap()

    d_ln1 = nc.dram_tensor("d_ln1", [8, 128, 32 * 512], BF16).ap()
    d_k = nc.dram_tensor("d_k", [DM, BT], BF16).ap()
    d_v = nc.dram_tensor("d_v", [BT, DM], BF16).ap()
    d_attn = nc.dram_tensor("d_attn", [QT, DM], BF16).ap()
    d_x1 = nc.dram_tensor("d_x1", [QT, DM], F32).ap()
    d_wg = nc.dram_tensor("d_wg", [II, DM], BF16).ap()
    d_wu = nc.dram_tensor("d_wu", [II, DM], BF16).ap()
    d_wd = nc.dram_tensor("d_wd", [DM, II], BF16).ap()
    d_h = nc.dram_tensor("d_h", [II, QT], BF16).ap()
    d_qmdump = nc.dram_tensor("d_qmdump", [128, 32 * 1024], BF16).ap()
    d_ln2dump = nc.dram_tensor("d_ln2dump", [128, 32 * 1024], BF16).ap()
    dbg_out = {}
    if dbg:
        for nm, t in (("dbg_ln1", d_ln1), ("dbg_k", d_k), ("dbg_v", d_v),
                      ("dbg_attn", d_attn), ("dbg_x1", d_x1), ("dbg_h", d_h),
                      ("dbg_wg", d_wg), ("dbg_wd", d_wd)):
            dt = F32 if nm == "dbg_x1" else BF16
            dbg_out[nm] = nc.dram_tensor(nm, list(t.tensor.shape), dt,
                                         kind="ExternalOutput").ap()
        for nm in ("dbg_qm", "dbg_ln2"):
            dbg_out[nm] = nc.dram_tensor(nm, [128, 32 * 1024], BF16,
                                         kind="ExternalOutput").ap()
        dbg_out["dbg_bh"] = nc.dram_tensor("dbg_bh", [128, NH * 128], BF16,
                                           kind="ExternalOutput").ap()
        dbg_out["dbg_wkt"] = nc.dram_tensor("dbg_wkt", [64, 16], F32,
                                            kind="ExternalOutput").ap()

    with ExitStack() as ctx:
        sb = lambda n, s, d: ctx.enter_context(nc.sbuf_tensor(n, s, d))
        ps = lambda n, s, d: ctx.enter_context(nc.psum_tensor(n, s, d))
        sem = lambda n: ctx.enter_context(nc.semaphore(n))

        s_bq = sb("s_bq", [128, 32], F32)
        s_bk = sb("s_bk", [128, 32], F32)
        s_bv = sb("s_bv", [1, DM], BF16)
        s_ones = sb("s_ones", [1, 128], BF16)
        s_idb = sb("s_idb", [128, 128], BF16)
        s_idf = sb("s_idf", [128, 128], F32)
        s_u = sb("s_u", [64, 128], F32)
        s_uw = sb("s_uw", [64, 128], F32)
        s_v = sb("s_v", [64, 128], F32)
        s_m = sb("s_m", [128, 128], F32)
        s_rev = sb("s_rev", [64, 1], F32)
        s_e = sb("s_e", [64, 1], F32)
        s_bwT = sb("s_bwT", [3, 16], F32)
        s_bk3 = sb("s_bk3", [3, 64], F32)
        s_wk16 = sb("s_wk16", [16, 64], F32)
        s_wkT = sb("s_wkT", [64, 16], F32)
        s_bh = sb("s_bh", [128, NH * 128], BF16)
        s_col = sb("s_col", [128, 8], F32)
        s_den = sb("s_den", [128, 16], F32)
        s_mx8 = sb("s_mx8", [128, 16], F32)
        s_dsum = sb("s_dsum", [128, 2], F32)
        s_rcp = sb("s_rcp", [128, 1], F32)
        s_ngm = sb("s_ngm", [128, 2], F32)
        s_ev = sb("s_ev", [128, 2 * 512], BF16)
        s_dqi = sb("s_dqi", [128, 2 * 2048], I16)
        s_dqo = sb("s_dqo", [128, 2 * 2048], BF16)
        s_zs = sb("s_zs", [128, 64], F32)
        s_nzs = sb("s_nzs", [128, 32], F32)
        s_qm = sb("s_qm", [128, 32 * 1024], BF16)
        A = sb("arena", [128, 49152], BF16)

        def cv(lo, hi, dt=BF16):
            v = A[:, lo:hi]
            return v if dt is BF16 else v.bitcast(dt)

        a_str = (cv(0, 16384), cv(16384, 32768))
        a_xt1 = s_qm[:, 0:8192].bitcast(F32)
        a_lt1 = s_qm[:, 8192:12288]
        a_sq1 = s_qm[:, 12288:16384]
        a_w8 = cv(32768, 49152)
        a_kh = cv(0, 16384)
        a_vh = cv(16384, 24576)
        a_lg2 = (cv(24576, 32768, F32), cv(32768, 40960, F32))
        a_atf = cv(40960, 45056)
        a_atT = cv(45056, 46080)
        a_ao = cv(46080, 46592)
        a_xt4 = cv(16384, 24576, F32)
        a_lt4 = cv(24576, 28672)
        a_lt4b = cv(28672, 32768)
        a_sq4 = cv(32768, 36864)
        a_x14 = cv(36864, 45056, F32)
        a_wt = (cv(0, 16384), cv(16384, 32768))
        a_gate = cv(32768, 36864)
        a_h5 = cv(36864, 38912)
        a_h2 = cv(0, 4096)
        a_rhs = cv(4096, 6144)
        a_lt7 = cv(6144, 10240)
        a_sq7 = cv(10240, 14336)
        a_x17 = cv(14336, 22528, F32)
        a_ot7 = cv(22528, 30720, F32)

        p0 = ps("p0", [128, 512], F32)
        p1 = ps("p1", [128, 512], F32)
        p2 = ps("p2", [128, 512], F32)
        p3 = ps("p3", [128, 512], F32)
        p4 = ps("p4", [128, 512], F32)
        p5 = ps("p5", [128, 512], F32)
        pt0 = ps("pt0", [128, 512], F32)
        pt1 = ps("pt1", [128, 512], F32)
        vt0 = pt0[:].bitcast(BF16)
        vt1 = pt1[:].bitcast(BF16)

        PE = sem("pe")
        DV = sem("dv")
        AC = sem("ac")
        PO = sem("po")
        PO2 = sem("po2")
        CON = sem("con")
        WQL = sem("wql")
        STRL = [sem("strl0"), sem("strl1")]
        STRS = [sem("strs0"), sem("strs1")]
        KVS = [sem("kvs0"), sem("kvs1")]
        KHL = sem("khl")
        VHL = sem("vhl")
        AOS = [sem("aos0"), sem("aos1")]
        XAL = sem("xal")
        X1S = [sem("x1s0"), sem("x1s1")]
        DQL = sem("dql")
        DQZ = sem("dqz")
        DQS = [sem("dqs0"), sem("dqs1")]
        WTL = [sem("wtl0"), sem("wtl1")]
        HS = [sem("hs0"), sem("hs1")]
        HL = [sem(f"hl{i}") for i in range(4)]
        RL = [sem(f"rl{i}") for i in range(4)]
        X1L = sem("x1l")
        OS = sem("os")

        pl = Plan()
        st_strip = Stream(pl, STRS)
        ld_strip = Stream(pl, STRL)
        st_kv = Stream(pl, KVS)
        st_ao = Stream(pl, AOS)
        st_x1 = Stream(pl, X1S)
        st_dq = Stream(pl, DQS)
        ld_wt = Stream(pl, WTL)
        st_h = Stream(pl, HS)
        ld_h = Stream(pl, HL)
        ld_r = Stream(pl, RL)

        slabs = []
        for r0 in range(0, II, 128):
            for g0 in range(0, NG_GU, 16):
                slabs.append((gq, gz, gs, d_wg, r0, g0, 16))
                slabs.append((uq, uz, us, d_wu, r0, g0, 16))
        for r0 in range(0, DM, 128):
            for g0 in range(0, NG_D, 16):
                ng = min(16, NG_D - g0)
                slabs.append((dq, dz, ds, d_wd, r0, g0, ng))
        n_gu_slabs = II // 128 * 4
        deq_state = {"i": 0, "po0": (None, 0), "po1": (None, 0)}
        deq_store_val = {}

        def pump_deq(n, eng=None):
            for k in range(n):
                idx = deq_state["i"]
                if idx >= len(slabs):
                    return
                deq_state["i"] = idx + 1
                q_in, z_in, s_in, w_out, r0, g0, ng = slabs[idx]
                b = idx % 2
                ncols = ng * GRP
                c0 = g0 * GRP
                dqi = s_dqi[:, b * 2048: b * 2048 + ncols]
                dqo = s_dqo[:, b * 2048: b * 2048 + ncols]
                zs = s_zs[:, b * 32: b * 32 + 2 * ng]
                ce = eng if eng else "vector"
                if eng == "alt":
                    ce = "vector" if idx % 2 == 0 else "scalar"
                ps_, pv_ = deq_state[f"po{b}"]
                if ps_ is not None:
                    pl.wait("sync", ps_, pv_)
                v_q = pl.dma("sync", dqi, q_in[r0:r0 + 128, c0:c0 + ncols], DQL)
                pl.dma("sync", zs[:, 0:ng], z_in[r0:r0 + 128, g0:g0 + ng], DQZ)
                v_z = pl.dma("sync", zs[:, ng:2 * ng], s_in[r0:r0 + 128, g0:g0 + ng],
                             DQZ)
                pl.wait(ce, DQL, v_q)
                pl.wait(ce, DQZ, v_z)
                st_dq.wait_done(ce, idx)
                if ce == "scalar":
                    nzs = s_nzs[:, b * 16: b * 16 + ng]
                    pl.wait("vector", DQZ, v_z)
                    v_nz = pl.op("vector", lambda e, zs=zs, ng=ng, nzs=nzs:
                                 e.scalar_tensor_tensor(nzs, zs[:, 0:ng], -1.0,
                                                        zs[:, ng:2 * ng],
                                                        ALU.mult, ALU.mult), DV)
                    pl.wait("scalar", DV, v_nz)
                    for g in range(ng):
                        pl.op("scalar",
                              lambda e, dqo=dqo, dqi=dqi, zs=zs, nzs=nzs, g=g, ng=ng:
                              e.activation(dqo[:, g * GRP:(g + 1) * GRP],
                                           dqi[:, g * GRP:(g + 1) * GRP],
                                           AF.Identity, bias=nzs[:, g:g + 1],
                                           scale=zs[:, ng + g:ng + g + 1]),
                              PO2 if g == ng - 1 else None)
                else:
                    for g in range(ng):
                        pl.op("vector", lambda e, dqo=dqo, dqi=dqi, zs=zs, g=g, ng=ng:
                              e.tensor_scalar(dqo[:, g * GRP:(g + 1) * GRP],
                                              dqi[:, g * GRP:(g + 1) * GRP],
                                              zs[:, g:g + 1], zs[:, ng + g:ng + g + 1],
                                              ALU.subtract, ALU.mult),
                              PO if g == ng - 1 else None)
                psem = PO2 if ce == "scalar" else PO
                deq_state[f"po{b}"] = (psem, pl.val(psem))
                pl.wait("gpsimd", psem, pl.val(psem))
                st_dq.dma("gpsimd", w_out[r0:r0 + 128, c0:c0 + ncols], dqo, idx)
                deq_store_val[idx] = (DQS[idx % 2], pl.val(DQS[idx % 2]))

        def wait_deq_upto(idx):
            if deq_state["i"] <= idx:
                pump_deq(idx + 1 - deq_state["i"])
            for k in (idx, idx - 1):
                if k >= 0 and k in deq_store_val:
                    s, v = deq_store_val[k]
                    pl.wait("sync", s, v)

        for dst, src in ((s_bq, bq2), (s_bk, bk2), (s_u, cons["c_u"]),
                         (s_v, cons["c_v"]), (s_m, cons["c_m"]),
                         (s_rev, cons["c_rev"]), (s_e, cons["c_e"]),
                         (s_idf, cons["c_id"]), (s_bwT, bwT), (s_bk3, cons["c_bk3"])):
            pl.dma("sync", dst[:], src, CON)
        v_con = pl.val(CON)
        pl.dma("gpsimd", s_idb[:], cons["c_id"], WQL)
        pl.dma("gpsimd", s_ones[:], cons["c_ones"], WQL)
        v_cast = pl.dma("gpsimd", s_bv[:], bv1, WQL)
        v_wk0 = []
        for i in range(4):
            v_wk0.append(pl.dma("gpsimd", a_w8[:, i * 4096:(i + 1) * 4096],
                                wkT[0][:, i * 4096:(i + 1) * 4096], WQL))

        pl.wait("tensor", CON, v_con)
        pl.wait("vector", CON, v_con)
        v = pl.op("tensor", lambda e: e.matmul(p0[:16, :64], s_bwT[:], s_bk3[:],
                                               start=True, stop=True), PE)
        pl.wait("vector", PE, v)
        v = pl.op("vector", lambda e: e.tensor_copy(s_wk16[:], p0[:16, :64]), DV)
        pl.wait("tensor", DV, v)
        v = pl.op("tensor", lambda e: e.transpose(p1[:64, :16], s_wk16[:],
                                                  s_idf[:16, :16]), PE)
        pl.wait("vector", PE, v)
        v = pl.op("vector", lambda e: e.tensor_scalar(
            s_wkT[:], p1[:64, :16], s_rev[:, 0:1], s_e[:, 0:1], ALU.mult, ALU.add), DV)
        pl.wait("tensor", DV, v)
        pl.wait("vector", DV, v)
        pl.wait("tensor", WQL, v_cast)
        v_mm_bh = 0
        for h in range(NH):
            pl.wait("vector", PE, v_mm_bh)
            v = pl.op("vector", lambda e, h=h: e.tensor_scalar(
                s_uw[:], s_u[:], s_wkT[:, h:h + 1], None, ALU.mult), DV)
            pl.wait("tensor", DV, v)
            v_mm_bh = pl.op("tensor", lambda e: e.matmul(
                p2[:, :128], s_uw[:], s_v[:], start=True, stop=True), PE)
            pl.wait("vector", PE, v_mm_bh)
            v = pl.op("vector", lambda e, h=h: e.tensor_mul(
                s_bh[:, h * 128:(h + 1) * 128], p2[:, :128], s_m[:]), DV)
        v_bh_done = pl.val(DV)

        def plan_ln_tile(src_ap, out_ap, sqs):
            cS, cMu, cE2, cNV, cRS, cNM = (s_col[:, i:i + 1] for i in range(6))
            v1 = pl.op("vector", lambda e: e.tensor_reduce(cS, src_ap, AX.X, ALU.add),
                       DV)
            v2 = pl.op("scalar", lambda e: e.activation(sqs, src_ap, AF.Square,
                                                        accum_out=cE2), AC)
            pl.wait("vector", DV, v1)
            pl.wait("vector", AC, v2)
            v = pl.op("vector", lambda e: e.tensor_scalar(cMu, cS, 1.0 / DM, None,
                                                          ALU.mult), DV)
            pl.wait("vector", DV, v)
            v = pl.op("vector", lambda e: e.tensor_scalar(cE2, cE2, 1.0 / DM, None,
                                                          ALU.mult), DV)
            pl.wait("vector", DV, v)
            v = pl.op("vector", lambda e: e.scalar_tensor_tensor(
                cNV, cMu, cMu, cE2, ALU.mult, ALU.subtract), DV)
            pl.wait("scalar", DV, v)
            v = pl.op("scalar", lambda e: e.activation(cRS, cNV, AF.Sqrt,
                                                       bias=np.float32(EPS), scale=np.float32(-1.0)),
                      AC)
            pl.wait("vector", AC, v)
            v = pl.op("vector", lambda e: e.reciprocal(cRS, cRS), DV)
            pl.wait("vector", DV, v)
            v = pl.op("vector", lambda e: e.scalar_tensor_tensor(
                cNM, cMu, -1.0, cRS, ALU.mult, ALU.mult), DV)
            pl.wait("scalar", DV, v)
            return pl.op("scalar", lambda e: e.activation(
                out_ap, src_ap, AF.Identity, bias=cNM, scale=cRS), AC)

        last_evac = {"pt0": 0, "pt1": 0}

        def plan_tok_to_fm(tok_ap, v_ready_ac, dst_fn):
            pl.wait("tensor", AC, v_ready_ac)
            for cg in range(8):
                bname = ("pt0", "pt1")[cg % 2]
                bank = (vt0, vt1)[cg % 2]
                pl.wait("tensor", DV, last_evac[bname])
                for j in range(4):
                    cc = cg * 4 + j
                    pl.op("tensor", lambda e, bank=bank, j=j, cc=cc, tok=tok_ap:
                          e.transpose(bank[:, j * 128:(j + 1) * 128],
                                      tok[:, cc * 128:(cc + 1) * 128], s_idb[:]),
                          PE if j == 3 else None)
                v_pe = pl.val(PE)
                pl.wait("vector", PE, v_pe)
                last_evac[bname] = pl.op(
                    "vector", lambda e, cg=cg, bank=bank: dst_fn(e, cg, bank[:, 0:512]),
                    DV)
            return pl.val(DV)

        kv_cnt_box = [0]
        strip_buf_pe = [0, 0]

        def emit_k_ocs(e8, strip, lbuf, v_ws=None):
            for oc in range(4):
                og = e8 * 4 + oc
                bank = (p0, p1, p2, p3)[oc % 4]
                pl.wait("tensor", AC, max(0, pl.val(AC) - 3))
                for cc in range(32):
                    if v_ws is not None and strip == 0 and oc == 0 and cc % 8 == 0:
                        pl.wait("tensor", WQL, v_ws[cc // 8])
                    pl.op("tensor", lambda e, bank=bank, cc=cc, oc=oc, lbuf=lbuf:
                          e.matmul(bank[:],
                                   a_w8[:, cc * 512 + oc * 128:cc * 512 + oc * 128 + 128],
                                   lbuf[:, cc * 512:(cc + 1) * 512],
                                   start=(cc == 0), stop=(cc == 31)),
                          PE if cc == 31 else None)
                v_pe = pl.val(PE)
                kv_cnt = kv_cnt_box[0]
                eb = s_ev[:, (kv_cnt % 2) * 512:(kv_cnt % 2) * 512 + 512]
                pl.wait("scalar", PE, v_pe)
                st_kv.wait_done("scalar", kv_cnt)
                v_ac = pl.op("scalar", lambda e, bank=bank, eb=eb, og=og:
                             e.activation(eb, bank[:], AF.Identity,
                                          bias=s_bk[:, og:og + 1], scale=1.0), AC)
                pl.wait("gpsimd", AC, v_ac)
                st_kv.dma("gpsimd", d_k[og * 128:(og + 1) * 128,
                                        strip * 512:(strip + 1) * 512], eb, kv_cnt)
                kv_cnt_box[0] = kv_cnt + 1

        pl.wait("scalar", CON, v_con)
        pl.wait("tensor", DV, v_bh_done)
        v_apply = None
        v_lt_pe = 0
        for strip in range(8):
            sbh = a_str[strip % 2]
            st_strip.wait_done("vector", strip)
            pl.wait("vector", PE, strip_buf_pe[strip % 2])
            for tt in range(4):
                t0 = strip * 512 + tt * 128
                if v_apply is not None:
                    pl.wait("sync", AC, v_apply)
                v_x = pl.dma("sync", a_xt1, xb[t0:t0 + 128, :], XAL)
                pl.wait("vector", XAL, v_x)
                pl.wait("scalar", XAL, v_x)
                pl.wait("scalar", PE, v_lt_pe)
                v_apply = plan_ln_tile(a_xt1, a_lt1, a_sq1)

                def dst_strip(e, cg, bank, tt=tt, sbh=sbh):
                    dst = sbh.rearrange("p (cc q) -> p cc q", q=512)[
                        :, cg * 4:(cg + 1) * 4, tt * 128:(tt + 1) * 128]
                    src = bank.rearrange("p (j q) -> p j q", q=128)
                    return e.tensor_copy(dst, src)
                plan_tok_to_fm(a_lt1, v_apply, dst_strip)
                v_lt_pe = pl.val(PE)
            v_dv = pl.val(DV)
            pl.wait("gpsimd", DV, v_dv)
            st_strip.dma("gpsimd", d_ln1[strip], sbh, strip)
            pl.wait("tensor", DV, v_dv)
            emit_k_ocs(0, strip, sbh, v_ws=v_wk0)
            strip_buf_pe[strip % 2] = pl.val(PE)
            pump_deq(2)

        def load_w_eighth(wT_src, e8):
            pl.wait("gpsimd", PE, pl.val(PE))
            vs = []
            for i in range(4):
                vs.append(pl.dma("gpsimd", a_w8[:, i * 4096:(i + 1) * 4096],
                                 wT_src[e8][:, i * 4096:(i + 1) * 4096], WQL))
            return vs

        def load_ln1_strip(strip):
            bi = strip % 2
            pl.wait("sync", PE, strip_buf_pe[bi])
            ld_strip.dma("sync", a_str[bi][:], d_ln1[strip], strip)
            return a_str[bi], bi

        st_strip.wait_all("sync")
        pl.wait("scalar", CON, v_con)
        pl.wait("gpsimd", AC, pl.val(AC))

        for e8 in range(1, 8):
            v_ws = load_w_eighth(wkT, e8)
            for strip in range(8):
                lbuf, bi = load_ln1_strip(strip)
                pl.wait("tensor", STRL[bi], pl.val(STRL[bi]))
                emit_k_ocs(e8, strip, lbuf, v_ws=v_ws if strip == 0 else None)
                strip_buf_pe[bi] = pl.val(PE)
                pump_deq(2)

        for e8 in range(8):
            v_ws = load_w_eighth(wvT, e8)
            o0 = e8 * 512
            for strip in range(8):
                lbuf, bi = load_ln1_strip(strip)
                pl.wait("tensor", STRL[bi], pl.val(STRL[bi]))
                for ts in range(4):
                    t0 = strip * 512 + ts * 128
                    bank = (p0, p1, p2, p3)[ts % 4]
                    pl.wait("tensor", DV, max(0, pl.val(DV) - 3))
                    for cc in range(32):
                        if strip == 0 and ts == 0 and cc % 8 == 0:
                            pl.wait("tensor", WQL, v_ws[cc // 8])
                        pl.op("tensor", lambda e, bank=bank, cc=cc, ts=ts, lbuf=lbuf:
                              e.matmul(bank[:],
                                       lbuf[:, cc * 512 + ts * 128:cc * 512 + ts * 128 + 128],
                                       a_w8[:, cc * 512:(cc + 1) * 512],
                                       start=(cc == 0), stop=False), None)
                    pl.op("tensor", lambda e, bank=bank, o0=o0:
                          e.matmul(bank[:], s_ones[0:1, :], s_bv[0:1, o0:o0 + 512],
                                   start=False, stop=True), PE)
                    v_pe = pl.val(PE)
                    kv_cnt = kv_cnt_box[0]
                    eb = s_ev[:, (kv_cnt % 2) * 512:(kv_cnt % 2) * 512 + 512]
                    pl.wait("vector", PE, v_pe)
                    st_kv.wait_done("vector", kv_cnt)
                    v_dv = pl.op("vector", lambda e, bank=bank, eb=eb:
                                 e.tensor_copy(eb, bank[:]), DV)
                    pl.wait("gpsimd", DV, v_dv)
                    st_kv.dma("gpsimd", d_v[t0:t0 + 128, o0:o0 + 512], eb, kv_cnt)
                    kv_cnt_box[0] = kv_cnt + 1
                strip_buf_pe[bi] = pl.val(PE)
                pump_deq(2)

        pl.wait("tensor", DV, v_bh_done)
        q_eb_pe = [0, 0]
        for e8 in range(8):
            v_ws = load_w_eighth(wqT, e8)
            for strip in range(2):
                lbuf, bi = load_ln1_strip(strip)
                pl.wait("tensor", STRL[bi], pl.val(STRL[bi]))
                for oc in range(4):
                    og = e8 * 4 + oc
                    head = og // 2
                    bank = (p0, p1)[oc % 2]
                    pl.wait("tensor", AC, max(0, pl.val(AC) - 1))
                    for cc in range(32):
                        if strip == 0 and oc == 0 and cc % 8 == 0:
                            pl.wait("tensor", WQL, v_ws[cc // 8])
                        pl.op("tensor", lambda e, bank=bank, cc=cc, oc=oc, lbuf=lbuf:
                              e.matmul(bank[:],
                                       a_w8[:, cc * 512 + oc * 128:cc * 512 + oc * 128 + 128],
                                       lbuf[:, cc * 512:(cc + 1) * 512],
                                       start=(cc == 0), stop=(cc == 31)),
                              PE if cc == 31 else None)
                    v_pe = pl.val(PE)
                    eb = s_ev[:, (oc % 2) * 512:(oc % 2) * 512 + 512]
                    pl.wait("scalar", PE, v_pe)
                    pl.wait("scalar", PE, q_eb_pe[oc % 2])
                    v_ac = pl.op("scalar", lambda e, bank=bank, eb=eb, og=og:
                                 e.activation(eb, bank[:], AF.Identity,
                                              bias=s_bq[:, og:og + 1], scale=1.0), AC)
                    pl.wait("tensor", AC, v_ac)
                    v_tr = pl.op("tensor", lambda e, eb=eb, head=head:
                                 e.matmul(p4[:], s_bh[:, head * 128:(head + 1) * 128],
                                          eb, start=True, stop=True), PE)
                    q_eb_pe[oc % 2] = v_tr
                    pl.wait("vector", PE, v_tr)
                    pl.op("vector", lambda e, og=og, strip=strip:
                          e.tensor_copy(s_qm[:, og * 1024 + strip * 512:
                                             og * 1024 + (strip + 1) * 512], p4[:]),
                          DV)
                strip_buf_pe[bi] = pl.val(PE)
            pump_deq(2)
        v_qmod_done = pl.val(DV)
        if dbg:
            pl.wait("sync", DV, v_qmod_done)
            pl.dma("sync", d_qmdump, s_qm[:], X1L)

        st_kv.wait_all("sync")
        pl.wait("tensor", DV, v_qmod_done)
        v_pe_p3 = pl.val(PE)
        kh_pe = [v_pe_p3, v_pe_p3]
        vh_pe = [v_pe_p3]
        ao_cnt = 0
        v_atf_pe = 0
        v_dsum_prev = [0, 0]
        lg_busy_ac = [0, 0]
        lgevac = {}
        v_pv_evac = 0
        v_ngm_prev = [0, 0]
        kv_loads = {}

        def emit_k_load(h):
            kb = h % 2
            khbuf = a_kh[:, kb * (2 * BT):(kb + 1) * (2 * BT)]
            pl.wait("sync", PE, kh_pe[kb])
            v_k = None
            for fc in range(2):
                v_k = pl.dma("sync", khbuf[:, fc * BT:(fc + 1) * BT],
                             d_k[h * 256 + fc * 128:h * 256 + (fc + 1) * 128, :], KHL)
            kv_loads[h] = (v_k, khbuf)

        def emit_v_load(h):
            pl.wait("sync", PE, vh_pe[0])
            srcv = d_v.rearrange("(ms p) f -> p ms f", p=128)[
                :, :, h * 256:(h + 1) * 256]
            dstv = a_vh.rearrange("p (ms f) -> p ms f", f=256)
            return pl.dma("sync", dstv, srcv, VHL)

        def pass1(hq):
            h, qt = divmod(hq, 8)
            lq = qt % 2
            lg = a_lg2[lq]
            v_k, khbuf = kv_loads[h]
            if qt == 0:
                pl.wait("tensor", KHL, v_k)
            for mt in range(8):
                bank = (p0, p1, p2, p3)[mt % 4]
                pl.wait("tensor", AC, lgevac.get(id(bank), 0))
                for fc in range(2):
                    pl.op("tensor",
                          lambda e, bank=bank, fc=fc, qt=qt, h=h, mt=mt, khbuf=khbuf:
                          e.matmul(bank[:],
                                   s_qm[:, (h * 2 + fc) * 1024 + qt * 128:
                                        (h * 2 + fc) * 1024 + qt * 128 + 128],
                                   khbuf[:, fc * BT + mt * 512:
                                         fc * BT + (mt + 1) * 512],
                                   start=(fc == 0), stop=(fc == 1)),
                          PE if fc == 1 else None)
                v_pe = pl.val(PE)
                pl.wait("scalar", PE, v_pe)
                if mt == 0:
                    pl.wait("scalar", AC, lg_busy_ac[lq])
                v_ev = pl.op("scalar", lambda e, bank=bank, mt=mt, lg=lg:
                             e.copy(lg[:, mt * 512:(mt + 1) * 512], bank[:]), AC)
                lgevac[id(bank)] = v_ev
                pl.wait("vector", AC, v_ev)
                if mt == 0:
                    pl.wait("vector", DV, v_ngm_prev[lq])
                pl.op("vector", lambda e, mt=mt, lq=lq, lg=lg: e.tensor_reduce(
                    s_mx8[:, lq * 8 + mt:lq * 8 + mt + 1],
                    lg[:, mt * 512:(mt + 1) * 512], AX.X, ALU.max, negate=True), DV)
            pl.wait("vector", DV, pl.val(DV))
            v_ngm = pl.op("vector", lambda e, lq=lq: e.tensor_reduce(
                s_ngm[:, lq:lq + 1], s_mx8[:, lq * 8:lq * 8 + 8], AX.X, ALU.min), DV)
            v_ngm_prev[lq] = v_ngm
            if qt == 7:
                kh_pe[h % 2] = pl.val(PE)
            return v_ngm

        def pass2(hq, v_ngm):
            nonlocal ao_cnt, v_atf_pe, v_pv_evac
            h, qt = divmod(hq, 8)
            lq = qt % 2
            lg = a_lg2[lq]
            if qt == 0:
                kv_loads[(h, "v")] = emit_v_load(h)
            v_v = kv_loads[(h, "v")]
            pl.wait("tensor", VHL, v_v)
            v_exps = [0] * 8
            trev_done = [0, 0]
            pv_done = [0, 0]
            tr_pe = [0] * 8

            def emit_exp(mt):
                if mt == 0:
                    pl.wait("scalar", DV, v_ngm)
                    pl.wait("scalar", DV, v_dsum_prev[lq])
                    pl.wait("scalar", PE, v_atf_pe)
                v_exps[mt] = pl.op(
                    "scalar", lambda e, lg=lg, lq=lq, mt=mt:
                    e.activation(a_atf[:, mt * 512:(mt + 1) * 512],
                                 lg[:, mt * 512:(mt + 1) * 512],
                                 AF.Exp, bias=s_ngm[:, lq:lq + 1], scale=1.0,
                                 accum_out=s_den[:, lq * 8 + mt:lq * 8 + mt + 1]),
                    AC)

            def emit_tr(mt):
                bank = (vt0, vt1)[mt % 2]
                bname = ("pt0", "pt1")[mt % 2]
                pl.wait("tensor", AC, v_exps[mt])
                pl.wait("tensor", DV, max(trev_done[mt % 2], last_evac[bname]))
                for j in range(4):
                    pl.op("tensor", lambda e, bank=bank, j=j, mt=mt:
                          e.transpose(bank[:, j * 128:(j + 1) * 128],
                                      a_atf[:, mt * 512 + j * 128:
                                            mt * 512 + (j + 1) * 128], s_idb[:]),
                          PE if j == 3 else None)
                tr_pe[mt] = pl.val(PE)
                pl.wait("vector", PE, tr_pe[mt])
                pl.wait("vector", PE, pv_done[mt % 2])
                tb = a_atT[:, (mt % 2) * 512:(mt % 2) * 512 + 512]
                v_dv = pl.op("vector", lambda e, tb=tb, bank=bank:
                             e.tensor_copy(tb, bank[:, 0:512]), DV)
                trev_done[mt % 2] = v_dv
                last_evac[bname] = v_dv
                return v_dv

            def emit_pv(mt, v_evac):
                tb = a_atT[:, (mt % 2) * 512:(mt % 2) * 512 + 512]
                pl.wait("tensor", DV, v_evac)
                if mt == 0:
                    pl.wait("tensor", DV, v_pv_evac)
                for j in range(4):
                    pl.op("tensor", lambda e, tb=tb, j=j, mt=mt:
                          e.matmul(p4[:, 0:256], tb[:, j * 128:(j + 1) * 128],
                                   a_vh[:, (mt * 4 + j) * 256:
                                        (mt * 4 + j + 1) * 256],
                                   start=(mt == 0 and j == 0),
                                   stop=(mt == 7 and j == 3)),
                          PE if j == 3 else None)
                pv_done[mt % 2] = pl.val(PE)

            emit_exp(0)
            emit_exp(1)
            evs = [0] * 8
            for mt in range(8):
                if mt + 2 < 8:
                    emit_exp(mt + 2)
                evs[mt] = emit_tr(mt)
                if mt >= 1:
                    emit_pv(mt - 1, evs[mt - 1])
            emit_pv(7, evs[7])
            v_pv_all = pl.val(PE)
            v_atf_pe = v_pv_all
            if qt == 7:
                vh_pe[0] = v_pv_all
            pl.wait("vector", AC, v_exps[7])
            v_ds = pl.op("vector", lambda e, lq=lq: e.tensor_reduce(
                s_dsum[:, lq:lq + 1], s_den[:, lq * 8:lq * 8 + 8], AX.X, ALU.add), DV)
            v_dsum_prev[lq] = v_ds
            pl.wait("vector", DV, v_ds)
            v_rc = pl.op("vector", lambda e, lq=lq:
                         e.reciprocal(s_rcp[:], s_dsum[:, lq:lq + 1]), DV)
            pl.wait("vector", DV, v_rc)
            pl.wait("vector", PE, v_pv_all)
            aob = a_ao[:, (ao_cnt % 2) * HD:(ao_cnt % 2) * HD + HD]
            st_ao.wait_done("vector", ao_cnt)
            v_pv_evac = pl.op("vector", lambda e, aob=aob: e.tensor_scalar(
                aob, p4[:, 0:256], s_rcp[:, 0:1], None, ALU.mult), DV)
            pl.wait("scalar", DV, v_pv_evac)
            st_ao.dma("scalar", d_attn[qt * 128:(qt + 1) * 128,
                                       h * 256:(h + 1) * 256], aob, ao_cnt)
            ao_cnt += 1

        emit_k_load(0)
        ngms = {0: pass1(0)}
        for hq in range(NH * 8):
            h2 = (hq + 2) // 8
            if h2 < NH and h2 not in kv_loads:
                emit_k_load(h2)
            if hq + 1 < NH * 8:
                ngms[hq + 1] = pass1(hq + 1)
            pass2(hq, ngms[hq])
            pump_deq(2, eng="alt")

        st_ao.wait_all("sync")
        pl.wait("sync", AC, pl.val(AC))
        v_pe_p4 = pl.val(PE)
        v_ac_p4 = pl.val(AC)
        blocks = [list(range(NG_D))[i:i + 4] for i in range(0, NG_D, 4)]
        wait_deq_upto(4 * 4 - 1)
        wt0 = a_wt[0][:, 0:32 * 512]
        v_wt0 = None
        for cc in range(32):
            v_wt0 = ld_wt.dma("sync", wt0[:, cc * 512:(cc + 1) * 512],
                              d_wg[0:512, cc * 128:(cc + 1) * 128], 0,
                              transpose=True)
        v_apply2 = None
        v_add1 = 0
        for tt in range(8):
            t0 = tt * 128
            pl.wait("sync", PE, pl.val(PE))
            if v_apply2 is not None:
                pl.wait("sync", AC, v_apply2)
                pl.wait("sync", DV, v_add1)
            pl.dma("sync", a_lt4, d_attn[t0:t0 + 128, :], XAL)
            v_x = pl.dma("sync", a_xt4, xb[t0:t0 + 128, :], XAL)
            pl.wait("vector", XAL, v_x)
            pl.wait("scalar", XAL, v_x)
            v_apply = plan_ln_tile(a_lt4, a_lt4b, a_sq4)
            pl.wait("vector", AC, v_apply)
            st_x1.wait_done("vector", tt)
            v_add1 = pl.op("vector", lambda e: e.tensor_add(a_x14, a_xt4, a_lt4b), DV)
            pl.wait("gpsimd", DV, v_add1)
            st_x1.dma("gpsimd", d_x1[t0:t0 + 128, :], a_x14, tt)
            pl.wait("vector", DV, v_add1)
            pl.wait("scalar", DV, v_add1)
            v_apply2 = plan_ln_tile(a_x14, a_lt4, a_sq4)

            def dst_qm(e, cg, bank, tt=tt):
                dst = s_qm.rearrange("p (cc q) -> p cc q", q=1024)[
                    :, cg * 4:(cg + 1) * 4, tt * 128:(tt + 1) * 128]
                src = bank.rearrange("p (j q) -> p j q", q=128)
                return e.tensor_copy(dst, src)
            plan_tok_to_fm(a_lt4, v_apply2, dst_qm)
            pump_deq(2)
            if tt == 3:
                v_ln2_half = pl.val(DV)
        v_ln2_done = pl.val(DV)
        if dbg:
            pl.wait("sync", DV, v_ln2_done)
            pl.dma("sync", d_ln2dump, s_qm[:], X1L)

        pl.wait("sync", AC, pl.val(AC))
        pl.wait("sync", DV, pl.val(DV))
        v_pe_p5 = pl.val(PE)
        wq_pe = [v_pe_p5, v_pe_p5]
        h_cnt = 0
        for bi2, blk in enumerate(blocks):
            nbo = len(blk) * 128
            for which, wsrc in ((0, d_wg), (1, d_wu)):
                wb = which
                wtile = a_wt[wb][:, 0:32 * nbo]
                if bi2 == 0 and which == 0:
                    pass
                else:
                    wait_deq_upto(min((blk[-1] + 1) * 4, n_gu_slabs) - 1)
                    pl.wait("sync", PE, wq_pe[wb])
                    for cc in range(32):
                        src = wsrc[blk[0] * 128: blk[0] * 128 + nbo,
                                   cc * 128:(cc + 1) * 128]
                        ld_wt.dma("sync", wtile[:, cc * nbo:(cc + 1) * nbo], src,
                                  wb, transpose=True)
                pl.wait("tensor", WTL[wb], pl.val(WTL[wb]))
                if which == 0:
                    pl.wait("scalar", DV, pl.val(DV))
                for j, oc in enumerate(blk):
                    for t2 in range(2):
                        bank = (p0, p1, p2, p3)[(j * 2 + t2) % 4]
                        if bi2 == 0 and which == 0:
                            pl.wait("tensor", DV,
                                    v_ln2_half if t2 == 0 else v_ln2_done)
                        elif bi2 == 0 and j == 0 and t2 == 0:
                            pl.wait("tensor", DV, v_ln2_done)
                        if which == 0:
                            pl.wait("tensor", AC, max(0, pl.val(AC) - 3))
                        else:
                            pl.wait("tensor", DV, max(0, pl.val(DV) - 3))
                        for cc in range(32):
                            pl.op("tensor",
                                  lambda e, bank=bank, cc=cc, j=j, t2=t2,
                                  wtile=wtile, nbo=nbo:
                                  e.matmul(bank[:],
                                           wtile[:, cc * nbo + j * 128:
                                                 cc * nbo + (j + 1) * 128],
                                           s_qm[:, cc * 1024 + t2 * 512:
                                                cc * 1024 + (t2 + 1) * 512],
                                           start=(cc == 0), stop=(cc == 31)),
                                  PE if cc == 31 else None)
                        v_pe = pl.val(PE)
                        if which == 0:
                            pl.wait("scalar", PE, v_pe)
                            pl.op("scalar", lambda e, bank=bank, j=j, t2=t2:
                                  e.activation(a_gate[:, j * 1024 + t2 * 512:
                                                      j * 1024 + (t2 + 1) * 512],
                                               bank[:], AF.Silu), AC)
                        else:
                            pl.wait("vector", PE, v_pe)
                            pl.wait("vector", AC, pl.val(AC))
                            st_h.wait_done("vector", h_cnt)
                            pl.op("vector",
                                  lambda e, bank=bank, j=j, t2=t2, hc=h_cnt:
                                  e.tensor_mul(a_h5[:, (hc % 2) * 1024 + t2 * 512:
                                                    (hc % 2) * 1024 + (t2 + 1) * 512],
                                               bank[:],
                                               a_gate[:, j * 1024 + t2 * 512:
                                                      j * 1024 + (t2 + 1) * 512]),
                                  DV)
                    if which == 1:
                        pl.wait("gpsimd", DV, pl.val(DV))
                        hb = a_h5[:, (h_cnt % 2) * 1024:(h_cnt % 2) * 1024 + 1024]
                        st_h.dma("gpsimd", d_h[oc * 128:(oc + 1) * 128, :], hb, h_cnt)
                        h_cnt += 1
                wq_pe[wb] = pl.val(PE)
            pump_deq(10)

        pump_deq(len(slabs))
        st_dq.wait_all("sync")
        st_h.wait_all("sync")
        banks8 = (p0, p1, p2, p3, p4, p5, pt0, pt1)
        v_pe_p6 = pl.val(PE)
        hl_pe = [v_pe_p6] * 4
        v_dv_p6 = pl.val(DV)
        for sl in range(8):
            for ic in range(NG_D):
                b = ic % 4
                hb = a_h2[:, b * 1024:b * 1024 + 1024]
                rb = a_rhs[:, b * 512:b * 512 + 512]
                pl.wait("sync", PE, hl_pe[b])
                ld_h.dma("sync", hb, d_h[ic * 128:(ic + 1) * 128, :], ic)
                ld_r.dma("sync", rb,
                         d_wd[sl * 512:(sl + 1) * 512, ic * 128:(ic + 1) * 128],
                         ic, transpose=True)
                pl.wait("tensor", HL[b], pl.val(HL[b]))
                pl.wait("tensor", RL[b], pl.val(RL[b]))
                if ic == 0:
                    pl.wait("tensor", DV, v_dv_p6)
                for tt in range(8):
                    pl.op("tensor", lambda e, tt=tt, hb=hb, rb=rb, ic=ic:
                          e.matmul(banks8[tt][:],
                                   hb[:, tt * 128:(tt + 1) * 128], rb,
                                   start=(ic == 0), stop=(ic == NG_D - 1)),
                          PE if tt == 7 else None)
                hl_pe[b] = pl.val(PE)
            v_pe = pl.val(PE)
            pl.wait("vector", PE, v_pe)
            for tt in range(8):
                pl.op("vector", lambda e, tt=tt, sl=sl:
                      e.tensor_copy(s_qm[:, tt * 4096 + sl * 512:
                                         tt * 4096 + (sl + 1) * 512],
                                    banks8[tt][:]), DV)
            v_dv_p6 = pl.val(DV)
        v_down_done = pl.val(DV)

        pl.wait("scalar", DV, v_down_done)
        pl.wait("vector", DV, v_down_done)
        pl.wait("scalar", PE, pl.val(PE))
        pl.wait("sync", PE, pl.val(PE))
        v_apply3 = None
        v_add_prev = 0
        n_out = 0
        for tt in range(8):
            t0 = tt * 128
            dtile = s_qm[:, tt * 4096:(tt + 1) * 4096]
            pl.wait("sync", DV, v_add_prev)
            v_x1 = pl.dma("sync", a_x17, d_x1[t0:t0 + 128, :], X1L)
            if v_apply3 is not None:
                pl.wait("vector", OS, 16 * tt)
                pl.wait("scalar", DV, v_add_prev)
            v_apply3 = plan_ln_tile(dtile, a_lt7, a_sq7)
            pl.wait("vector", AC, v_apply3)
            pl.wait("vector", X1L, v_x1)
            v_add_prev = pl.op("vector",
                               lambda e: e.tensor_add(a_ot7, a_x17, a_lt7), DV)
            pl.wait("sync", DV, v_add_prev)
            pl.dma("sync", out[t0:t0 + 128, :], a_ot7, OS)
            n_out += 1
        pl.wait("sync", OS, 16 * n_out)
        pl.wait("gpsimd", OS, 16 * n_out)

        if dbg:
            DGS = X1L
            for nm, src in (("dbg_ln1", d_ln1), ("dbg_k", d_k), ("dbg_v", d_v),
                            ("dbg_attn", d_attn), ("dbg_x1", d_x1), ("dbg_h", d_h),
                            ("dbg_wg", d_wg), ("dbg_wd", d_wd)):
                pl.dma("sync", dbg_out[nm], src, DGS)
            pl.dma("sync", dbg_out["dbg_qm"], d_qmdump, DGS)
            pl.dma("sync", dbg_out["dbg_ln2"], d_ln2dump, DGS)
            pl.dma("sync", dbg_out["dbg_bh"], s_bh[:], DGS)
            pl.dma("sync", dbg_out["dbg_wkt"], s_wkT[:], DGS)
            pl.wait("sync", DGS, pl.val(DGS))

        block = ctx.enter_context(nc.Block())
        for name in ENGINES:
            fns = pl.ops[name]
            getattr(block, name)(lambda e, fns=fns: [fn(e) for fn in fns])

    return nc


def make_in_maps(x, Wq, bq, Wk, bk, Wv, bv, bias_weight,
                 gqw, gsc, gze, uqw, usc, uze, dqw, dsc, dze):
    def c(a):
        return np.ascontiguousarray(a)

    def as_i16(q):
        return c(np.asarray(q, dtype=np.int32).view(np.int16)[:, ::2])

    def tile8(W):
        wt = np.asarray(W, np.float32).T.reshape(32, 128, 8, 512)
        return c(wt.transpose(2, 1, 0, 3).reshape(8, 128, 32 * 512))

    common = dict(
        wqT=tile8(Wq), wkT=tile8(Wk), wvT=tile8(Wv),
        bq2=c(np.asarray(bq, np.float32).reshape(32, 128).T),
        bk2=c(np.asarray(bk, np.float32).reshape(32, 128).T),
        bv1=c(np.asarray(bv, np.float32).reshape(1, DM)),
        bwT=c(np.asarray(bias_weight, np.float32).T),
        gq=as_i16(gqw), uq=as_i16(uqw), dq=as_i16(dqw),
        gz=c(np.asarray(gze, np.float32)), gs=c(np.asarray(gsc, np.float32)),
        uz=c(np.asarray(uze, np.float32)), us=c(np.asarray(usc, np.float32)),
        dz=c(np.asarray(dze, np.float32)), ds=c(np.asarray(dsc, np.float32)),
        **{k: c(v) for k, v in host_constants().items()})
    in_maps = []
    for core in range(8):
        b, q = core // 4, core % 4
        xb_b = np.asarray(x[b], np.float32)
        perm = np.concatenate([xb_b[q * QT:(q + 1) * QT],
                               xb_b[:q * QT], xb_b[(q + 1) * QT:]], axis=0)
        in_maps.append(dict(common, xb=c(perm)))
    return in_maps


def kernel(L0_attn_x, L0_attn_Wq, L0_attn_bq, L0_attn_Wk, L0_attn_bk,
           L0_attn_Wv, L0_attn_bv, L0_attn_bias_weight,
           L0_ffn_gate_qweight, L0_ffn_gate_scales, L0_ffn_gate_zeros,
           L0_ffn_up_qweight, L0_ffn_up_scales, L0_ffn_up_zeros,
           L0_ffn_down_qweight, L0_ffn_down_scales, L0_ffn_down_zeros):
    x = np.asarray(L0_attn_x, dtype=np.float32)
    assert x.shape == (2, BT, DM)
    in_maps = make_in_maps(x, L0_attn_Wq, L0_attn_bq, L0_attn_Wk, L0_attn_bk,
                           L0_attn_Wv, L0_attn_bv, L0_attn_bias_weight,
                           L0_ffn_gate_qweight, L0_ffn_gate_scales,
                           L0_ffn_gate_zeros, L0_ffn_up_qweight,
                           L0_ffn_up_scales, L0_ffn_up_zeros,
                           L0_ffn_down_qweight, L0_ffn_down_scales,
                           L0_ffn_down_zeros)
    nc = build_program()
    res = run_bass_kernel_spmd(nc, in_maps, core_ids=list(range(8)))
    outp = np.empty((2, BT, DM), np.float32)
    for core in range(8):
        b, q = core // 4, core % 4
        outp[b, q * QT:(q + 1) * QT] = res.results[core]["out"]
    return outp


# version 15
# speedup vs baseline: 1.0602x; 1.0442x over previous
import numpy as np
import ml_dtypes
import concourse.bass as bass
import concourse.mybir as mybir
from concourse.bass_utils import run_bass_kernel_spmd
from contextlib import ExitStack

F32 = mybir.dt.float32
BF16 = mybir.dt.bfloat16
I16 = mybir.dt.int16
AF = mybir.ActivationFunctionType
ALU = mybir.AluOpType
AX = mybir.AxisListType

ENGINES = ("sync", "tensor", "vector", "scalar", "gpsimd")

NB, CD, NH = 8, 32, 16
HD = CD * NB
DM = 4096
BT = 4096
QT = 1024
II = 11008
GRP = 128
NG_GU = DM // GRP
NG_D = II // GRP
SCALE = 1.0 / np.sqrt(HD)
EPS = 1e-5


def _ga_tables():
    n = 8
    masks = sorted(range(n), key=lambda m: (bin(m).count("1"), m))
    pos = {m: k for k, m in enumerate(masks)}
    C = np.zeros((n, n, n), dtype=np.float64)
    for ai, a in enumerate(masks):
        for bi, b in enumerate(masks):
            swaps, at = 0, a >> 1
            while at:
                swaps += bin(at & b).count("1")
                at >>= 1
            C[ai, bi, pos[a ^ b]] = -1.0 if (swaps & 1) else 1.0
    grades = np.array([bin(m).count("1") for m in masks])
    rev = np.where((grades * (grades - 1) // 2) % 2 == 1, -1.0, 1.0)
    return C, rev


def host_constants():
    C, rev = _ga_tables()
    biv = C[:, :, [4, 5, 6]]
    bk3 = np.ascontiguousarray(biv.transpose(2, 0, 1).reshape(3, 64)).astype(np.float32)
    U = np.zeros((64, 128), np.float32)
    V = np.zeros((64, 128), np.float32)
    for i in range(8):
        for j in range(8):
            for d in range(16):
                U[i * 8 + j, d * 8 + i] = 1.0
                V[i * 8 + j, d * 8 + j] = 1.0
    M128 = np.zeros((128, 128), np.float32)
    for d in range(16):
        M128[d * 8:(d + 1) * 8, d * 8:(d + 1) * 8] = 1.0
    revcol = np.zeros((64, 1), np.float32)
    ecol = np.zeros((64, 1), np.float32)
    for i in range(8):
        for j in range(8):
            revcol[i * 8 + j, 0] = rev[i]
            if i == j:
                ecol[i * 8 + j, 0] = SCALE * rev[j]
    return dict(c_bk3=bk3, c_u=U, c_v=V, c_m=M128, c_rev=revcol, c_e=ecol,
                c_id=np.eye(128, dtype=np.float32),
                c_ones=np.ones((1, 128), np.float32))


class Plan:
    def __init__(self):
        self.ops = {e: [] for e in ENGINES}
        self.cnt = {}

    def inc(self, sem, n=1):
        c = self.cnt.get(sem.name, 0) + n
        self.cnt[sem.name] = c
        return c

    def val(self, sem):
        return self.cnt.get(sem.name, 0)

    def emit(self, eng, fn):
        self.ops[eng].append(fn)

    def wait(self, eng, sem, v):
        if v is None or v <= 0:
            return
        self.emit(eng, lambda e, s=sem, v=v: e.wait_ge(s, v))

    def wait_full(self, eng, sem):
        self.wait(eng, sem, self.val(sem))

    def op(self, eng, build, sem=None, n=1):
        if sem is None:
            self.emit(eng, build)
            return None
        v = self.inc(sem, n)
        self.emit(eng, lambda e, b=build, s=sem, n=n: b(e).then_inc(s, n))
        return v

    def dma(self, eng, out, in_, sem, transpose=False):
        v = self.inc(sem, 16)
        if transpose:
            self.emit(eng, lambda e, o=out, i=in_, s=sem:
                      e.dma_start(out=o, in_=i, transpose=True).then_inc(s, 16))
        else:
            self.emit(eng, lambda e, o=out, i=in_, s=sem:
                      e.dma_start(out=o, in_=i).then_inc(s, 16))
        return v


class Stream:

    def __init__(self, pl, sems):
        self.pl, self.sems, self.n = pl, sems, len(sems)

    def dma(self, eng, out, in_, idx, transpose=False):
        return self.pl.dma(eng, out, in_, self.sems[idx % self.n], transpose)

    def wait_done(self, eng, idx):
        s = self.sems[idx % self.n]
        self.pl.wait(eng, s, self.pl.val(s))

    def wait_all(self, eng):
        for s in self.sems:
            self.pl.wait(eng, s, self.pl.val(s))


def build_program(dbg=False):
    nc = bass.Bass("TRN2", target_bir_lowering=False)

    xb = nc.dram_tensor("xb", [BT, DM], F32, kind="ExternalInput").ap()
    wqT = nc.dram_tensor("wqT", [8, 128, 32 * 512], F32, kind="ExternalInput").ap()
    wkT = nc.dram_tensor("wkT", [8, 128, 32 * 512], F32, kind="ExternalInput").ap()
    wvT = nc.dram_tensor("wvT", [8, 128, 32 * 512], F32, kind="ExternalInput").ap()
    bq2 = nc.dram_tensor("bq2", [128, 32], F32, kind="ExternalInput").ap()
    bk2 = nc.dram_tensor("bk2", [128, 32], F32, kind="ExternalInput").ap()
    bv1 = nc.dram_tensor("bv1", [1, DM], F32, kind="ExternalInput").ap()
    bwT = nc.dram_tensor("bwT", [3, 16], F32, kind="ExternalInput").ap()
    gq = nc.dram_tensor("gq", [II, DM], I16, kind="ExternalInput").ap()
    uq = nc.dram_tensor("uq", [II, DM], I16, kind="ExternalInput").ap()
    dq = nc.dram_tensor("dq", [DM, II], I16, kind="ExternalInput").ap()
    gz = nc.dram_tensor("gz", [II, NG_GU], F32, kind="ExternalInput").ap()
    gs = nc.dram_tensor("gs", [II, NG_GU], F32, kind="ExternalInput").ap()
    uz = nc.dram_tensor("uz", [II, NG_GU], F32, kind="ExternalInput").ap()
    us = nc.dram_tensor("us", [II, NG_GU], F32, kind="ExternalInput").ap()
    dz = nc.dram_tensor("dz", [DM, NG_D], F32, kind="ExternalInput").ap()
    ds = nc.dram_tensor("ds", [DM, NG_D], F32, kind="ExternalInput").ap()
    cons = {k: nc.dram_tensor(k, list(v.shape), F32, kind="ExternalInput").ap()
            for k, v in host_constants().items()}
    out = nc.dram_tensor("out", [QT, DM], F32, kind="ExternalOutput").ap()

    d_ln1 = nc.dram_tensor("d_ln1", [8, 128, 32 * 512], BF16).ap()
    d_k = nc.dram_tensor("d_k", [DM, BT], BF16).ap()
    d_v = nc.dram_tensor("d_v", [BT, DM], BF16).ap()
    d_attn = nc.dram_tensor("d_attn", [QT, DM], BF16).ap()
    d_x1 = nc.dram_tensor("d_x1", [QT, DM], F32).ap()
    d_wg = nc.dram_tensor("d_wg", [II, DM], BF16).ap()
    d_wu = nc.dram_tensor("d_wu", [II, DM], BF16).ap()
    d_wd = nc.dram_tensor("d_wd", [DM, II], BF16).ap()
    d_h = nc.dram_tensor("d_h", [II, QT], BF16).ap()
    d_qmdump = nc.dram_tensor("d_qmdump", [128, 32 * 1024], BF16).ap()
    d_ln2dump = nc.dram_tensor("d_ln2dump", [128, 32 * 1024], BF16).ap()
    dbg_out = {}
    if dbg:
        for nm, t in (("dbg_ln1", d_ln1), ("dbg_k", d_k), ("dbg_v", d_v),
                      ("dbg_attn", d_attn), ("dbg_x1", d_x1), ("dbg_h", d_h),
                      ("dbg_wg", d_wg), ("dbg_wd", d_wd)):
            dt = F32 if nm == "dbg_x1" else BF16
            dbg_out[nm] = nc.dram_tensor(nm, list(t.tensor.shape), dt,
                                         kind="ExternalOutput").ap()
        for nm in ("dbg_qm", "dbg_ln2"):
            dbg_out[nm] = nc.dram_tensor(nm, [128, 32 * 1024], BF16,
                                         kind="ExternalOutput").ap()
        dbg_out["dbg_bh"] = nc.dram_tensor("dbg_bh", [128, NH * 128], BF16,
                                           kind="ExternalOutput").ap()
        dbg_out["dbg_wkt"] = nc.dram_tensor("dbg_wkt", [64, 16], F32,
                                            kind="ExternalOutput").ap()

    with ExitStack() as ctx:
        sb = lambda n, s, d: ctx.enter_context(nc.sbuf_tensor(n, s, d))
        ps = lambda n, s, d: ctx.enter_context(nc.psum_tensor(n, s, d))
        sem = lambda n: ctx.enter_context(nc.semaphore(n))

        s_bq = sb("s_bq", [128, 32], F32)
        s_bk = sb("s_bk", [128, 32], F32)
        s_bv = sb("s_bv", [1, DM], BF16)
        s_ones = sb("s_ones", [1, 128], BF16)
        s_idb = sb("s_idb", [128, 128], BF16)
        s_idf = sb("s_idf", [128, 128], F32)
        s_u = sb("s_u", [64, 128], F32)
        s_uw = sb("s_uw", [64, 128], F32)
        s_v = sb("s_v", [64, 128], F32)
        s_m = sb("s_m", [128, 128], F32)
        s_rev = sb("s_rev", [64, 1], F32)
        s_e = sb("s_e", [64, 1], F32)
        s_bwT = sb("s_bwT", [3, 16], F32)
        s_bk3 = sb("s_bk3", [3, 64], F32)
        s_wk16 = sb("s_wk16", [16, 64], F32)
        s_wkT = sb("s_wkT", [64, 16], F32)
        s_bh = sb("s_bh", [128, NH * 128], BF16)
        s_col = sb("s_col", [128, 8], F32)
        s_den = sb("s_den", [128, 16], F32)
        s_mx8 = sb("s_mx8", [128, 16], F32)
        s_dsum = sb("s_dsum", [128, 2], F32)
        s_rcp = sb("s_rcp", [128, 1], F32)
        s_ngm = sb("s_ngm", [128, 2], F32)
        s_ev = sb("s_ev", [128, 2 * 512], BF16)
        s_dqi = sb("s_dqi", [128, 2 * 2048], I16)
        s_dqo = sb("s_dqo", [128, 2 * 2048], BF16)
        s_zs = sb("s_zs", [128, 64], F32)
        s_nzs = sb("s_nzs", [128, 32], F32)
        s_qm = sb("s_qm", [128, 32 * 1024], BF16)
        A = sb("arena", [128, 49152], BF16)

        def cv(lo, hi, dt=BF16):
            v = A[:, lo:hi]
            return v if dt is BF16 else v.bitcast(dt)

        a_str = (cv(0, 16384), cv(16384, 32768))
        a_xt1 = s_qm[:, 0:8192].bitcast(F32)
        a_lt1 = s_qm[:, 8192:12288]
        a_sq1 = s_qm[:, 12288:16384]
        a_w8 = cv(32768, 49152)
        a_kh = cv(0, 16384)
        a_vh = cv(16384, 24576)
        a_lg2 = (cv(24576, 32768, F32), cv(32768, 40960, F32))
        a_atf = cv(40960, 45056)
        a_atT = cv(45056, 46080)
        a_ao = cv(46080, 46592)
        a_xt4 = cv(16384, 24576, F32)
        a_lt4 = cv(24576, 28672)
        a_lt4b = cv(28672, 32768)
        a_sq4 = cv(32768, 36864)
        a_x14 = cv(36864, 45056, F32)
        a_wt = (cv(0, 16384), cv(16384, 32768))
        a_gate = cv(32768, 36864)
        a_h5 = cv(36864, 38912)
        a_h2 = cv(0, 4096)
        a_rhs = cv(4096, 6144)
        a_lt7 = cv(6144, 10240)
        a_sq7 = cv(10240, 14336)
        a_x17 = cv(14336, 22528, F32)
        a_ot7 = cv(22528, 30720, F32)

        p0 = ps("p0", [128, 512], F32)
        p1 = ps("p1", [128, 512], F32)
        p2 = ps("p2", [128, 512], F32)
        p3 = ps("p3", [128, 512], F32)
        p4 = ps("p4", [128, 512], F32)
        p5 = ps("p5", [128, 512], F32)
        pt0 = ps("pt0", [128, 512], F32)
        pt1 = ps("pt1", [128, 512], F32)
        vt0 = pt0[:].bitcast(BF16)
        vt1 = pt1[:].bitcast(BF16)

        PE = sem("pe")
        DV = sem("dv")
        AC = sem("ac")
        PO = sem("po")
        PO2 = sem("po2")
        CON = sem("con")
        WQL = sem("wql")
        W8S = [sem(f"w8s{i}") for i in range(4)]
        STRL = [sem("strl0"), sem("strl1")]
        STRS = [sem("strs0"), sem("strs1")]
        KVS = [sem("kvs0"), sem("kvs1")]
        KHL2 = [sem("khl0"), sem("khl1")]
        VHL = sem("vhl")
        AOS = [sem("aos0"), sem("aos1")]
        XAL = sem("xal")
        X1S = [sem("x1s0"), sem("x1s1")]
        DQL2 = [sem("dql0"), sem("dql1")]
        DQZ2 = [sem("dqz0"), sem("dqz1")]
        DQS = [sem("dqs0"), sem("dqs1")]
        WTL = [sem("wtl0"), sem("wtl1")]
        HS = [sem("hs0"), sem("hs1")]
        HL = [sem(f"hl{i}") for i in range(4)]
        RL = [sem(f"rl{i}") for i in range(4)]
        X1L = sem("x1l")
        OS = sem("os")

        pl = Plan()
        st_strip = Stream(pl, STRS)
        ld_strip = Stream(pl, STRL)
        st_kv = Stream(pl, KVS)
        st_ao = Stream(pl, AOS)
        st_x1 = Stream(pl, X1S)
        st_dq = Stream(pl, DQS)
        ld_wt = Stream(pl, WTL)
        st_h = Stream(pl, HS)
        ld_h = Stream(pl, HL)
        ld_r = Stream(pl, RL)

        slabs = []
        for r0 in range(0, II, 128):
            for g0 in range(0, NG_GU, 16):
                slabs.append((gq, gz, gs, d_wg, r0, g0, 16))
                slabs.append((uq, uz, us, d_wu, r0, g0, 16))
        for r0 in range(0, DM, 128):
            for g0 in range(0, NG_D, 16):
                ng = min(16, NG_D - g0)
                slabs.append((dq, dz, ds, d_wd, r0, g0, ng))
        n_gu_slabs = II // 128 * 4
        deq_state = {"i": 0, "po0": (None, 0), "po1": (None, 0)}
        deq_store_val = {}

        def pump_deq(n, eng=None):
            for k in range(n):
                idx = deq_state["i"]
                if idx >= len(slabs):
                    return
                deq_state["i"] = idx + 1
                q_in, z_in, s_in, w_out, r0, g0, ng = slabs[idx]
                b = idx % 2
                ncols = ng * GRP
                c0 = g0 * GRP
                dqi = s_dqi[:, b * 2048: b * 2048 + ncols]
                dqo = s_dqo[:, b * 2048: b * 2048 + ncols]
                zs = s_zs[:, b * 32: b * 32 + 2 * ng]
                ce = eng if eng else "vector"
                if eng == "alt":
                    ce = "vector" if idx % 2 == 0 else "scalar"
                ps_, pv_ = deq_state[f"po{b}"]
                if ps_ is not None:
                    pl.wait("sync", ps_, pv_)
                v_q = pl.dma("sync", dqi, q_in[r0:r0 + 128, c0:c0 + ncols], DQL2[b])
                pl.dma("sync", zs[:, 0:ng], z_in[r0:r0 + 128, g0:g0 + ng], DQZ2[b])
                v_z = pl.dma("sync", zs[:, ng:2 * ng], s_in[r0:r0 + 128, g0:g0 + ng],
                             DQZ2[b])
                pl.wait(ce, DQL2[b], v_q)
                pl.wait(ce, DQZ2[b], v_z)
                st_dq.wait_done(ce, idx)
                if ce == "scalar":
                    nzs = s_nzs[:, b * 16: b * 16 + ng]
                    pl.wait("vector", DQZ2[b], v_z)
                    v_nz = pl.op("vector", lambda e, zs=zs, ng=ng, nzs=nzs:
                                 e.scalar_tensor_tensor(nzs, zs[:, 0:ng], -1.0,
                                                        zs[:, ng:2 * ng],
                                                        ALU.mult, ALU.mult), DV)
                    pl.wait("scalar", DV, v_nz)
                    for g in range(ng):
                        pl.op("scalar",
                              lambda e, dqo=dqo, dqi=dqi, zs=zs, nzs=nzs, g=g, ng=ng:
                              e.activation(dqo[:, g * GRP:(g + 1) * GRP],
                                           dqi[:, g * GRP:(g + 1) * GRP],
                                           AF.Identity, bias=nzs[:, g:g + 1],
                                           scale=zs[:, ng + g:ng + g + 1]),
                              PO2 if g == ng - 1 else None)
                else:
                    for g in range(ng):
                        pl.op("vector", lambda e, dqo=dqo, dqi=dqi, zs=zs, g=g, ng=ng:
                              e.tensor_scalar(dqo[:, g * GRP:(g + 1) * GRP],
                                              dqi[:, g * GRP:(g + 1) * GRP],
                                              zs[:, g:g + 1], zs[:, ng + g:ng + g + 1],
                                              ALU.subtract, ALU.mult),
                              PO if g == ng - 1 else None)
                psem = PO2 if ce == "scalar" else PO
                deq_state[f"po{b}"] = (psem, pl.val(psem))
                pl.wait("gpsimd", psem, pl.val(psem))
                st_dq.dma("gpsimd", w_out[r0:r0 + 128, c0:c0 + ncols], dqo, idx)
                deq_store_val[idx] = (DQS[idx % 2], pl.val(DQS[idx % 2]))

        def wait_deq_upto(idx):
            if deq_state["i"] <= idx:
                pump_deq(idx + 1 - deq_state["i"])
            for k in (idx, idx - 1):
                if k >= 0 and k in deq_store_val:
                    s, v = deq_store_val[k]
                    pl.wait("sync", s, v)

        for dst, src in ((s_bq, bq2), (s_bk, bk2), (s_u, cons["c_u"]),
                         (s_v, cons["c_v"]), (s_m, cons["c_m"]),
                         (s_rev, cons["c_rev"]), (s_e, cons["c_e"]),
                         (s_idf, cons["c_id"]), (s_bwT, bwT), (s_bk3, cons["c_bk3"])):
            pl.dma("sync", dst[:], src, CON)
        v_con = pl.val(CON)
        pl.dma("gpsimd", s_idb[:], cons["c_id"], WQL)
        pl.dma("gpsimd", s_ones[:], cons["c_ones"], WQL)
        v_cast = pl.dma("gpsimd", s_bv[:], bv1, WQL)
        v_wk0 = []
        for i in range(4):
            v_wk0.append((W8S[i], pl.dma("gpsimd", a_w8[:, i * 4096:(i + 1) * 4096],
                                         wkT[0][:, i * 4096:(i + 1) * 4096], W8S[i])))

        pl.wait("tensor", CON, v_con)
        pl.wait("vector", CON, v_con)
        v = pl.op("tensor", lambda e: e.matmul(p0[:16, :64], s_bwT[:], s_bk3[:],
                                               start=True, stop=True), PE)
        pl.wait("vector", PE, v)
        v = pl.op("vector", lambda e: e.tensor_copy(s_wk16[:], p0[:16, :64]), DV)
        pl.wait("tensor", DV, v)
        v = pl.op("tensor", lambda e: e.transpose(p1[:64, :16], s_wk16[:],
                                                  s_idf[:16, :16]), PE)
        pl.wait("vector", PE, v)
        v = pl.op("vector", lambda e: e.tensor_scalar(
            s_wkT[:], p1[:64, :16], s_rev[:, 0:1], s_e[:, 0:1], ALU.mult, ALU.add), DV)
        pl.wait("tensor", DV, v)
        pl.wait("vector", DV, v)
        pl.wait("tensor", WQL, v_cast)
        v_mm_bh = 0
        for h in range(NH):
            pl.wait("vector", PE, v_mm_bh)
            v = pl.op("vector", lambda e, h=h: e.tensor_scalar(
                s_uw[:], s_u[:], s_wkT[:, h:h + 1], None, ALU.mult), DV)
            pl.wait("tensor", DV, v)
            v_mm_bh = pl.op("tensor", lambda e: e.matmul(
                p2[:, :128], s_uw[:], s_v[:], start=True, stop=True), PE)
            pl.wait("vector", PE, v_mm_bh)
            v = pl.op("vector", lambda e, h=h: e.tensor_mul(
                s_bh[:, h * 128:(h + 1) * 128], p2[:, :128], s_m[:]), DV)
        v_bh_done = pl.val(DV)

        def plan_ln_tile(src_ap, out_ap, sqs):
            cS, cMu, cE2, cNV, cRS, cNM = (s_col[:, i:i + 1] for i in range(6))
            v1 = pl.op("vector", lambda e: e.tensor_reduce(cS, src_ap, AX.X, ALU.add),
                       DV)
            v2 = pl.op("scalar", lambda e: e.activation(sqs, src_ap, AF.Square,
                                                        accum_out=cE2), AC)
            pl.wait("vector", DV, v1)
            pl.wait("vector", AC, v2)
            v = pl.op("vector", lambda e: e.tensor_scalar(cMu, cS, 1.0 / DM, None,
                                                          ALU.mult), DV)
            pl.wait("vector", DV, v)
            v = pl.op("vector", lambda e: e.tensor_scalar(cE2, cE2, 1.0 / DM, None,
                                                          ALU.mult), DV)
            pl.wait("vector", DV, v)
            v = pl.op("vector", lambda e: e.scalar_tensor_tensor(
                cNV, cMu, cMu, cE2, ALU.mult, ALU.subtract), DV)
            pl.wait("scalar", DV, v)
            v = pl.op("scalar", lambda e: e.activation(cRS, cNV, AF.Sqrt,
                                                       bias=np.float32(EPS), scale=np.float32(-1.0)),
                      AC)
            pl.wait("vector", AC, v)
            v = pl.op("vector", lambda e: e.reciprocal(cRS, cRS), DV)
            pl.wait("vector", DV, v)
            v = pl.op("vector", lambda e: e.scalar_tensor_tensor(
                cNM, cMu, -1.0, cRS, ALU.mult, ALU.mult), DV)
            pl.wait("scalar", DV, v)
            return pl.op("scalar", lambda e: e.activation(
                out_ap, src_ap, AF.Identity, bias=cNM, scale=cRS), AC)

        last_evac = {"pt0": 0, "pt1": 0}

        def plan_tok_to_fm(tok_ap, v_ready_ac, dst_fn):
            pl.wait("tensor", AC, v_ready_ac)
            for cg in range(8):
                bname = ("pt0", "pt1")[cg % 2]
                bank = (vt0, vt1)[cg % 2]
                pl.wait("tensor", DV, last_evac[bname])
                for j in range(4):
                    cc = cg * 4 + j
                    pl.op("tensor", lambda e, bank=bank, j=j, cc=cc, tok=tok_ap:
                          e.transpose(bank[:, j * 128:(j + 1) * 128],
                                      tok[:, cc * 128:(cc + 1) * 128], s_idb[:]),
                          PE if j == 3 else None)
                v_pe = pl.val(PE)
                pl.wait("vector", PE, v_pe)
                last_evac[bname] = pl.op(
                    "vector", lambda e, cg=cg, bank=bank: dst_fn(e, cg, bank[:, 0:512]),
                    DV)
            return pl.val(DV)

        kv_cnt_box = [0]
        strip_buf_pe = [0, 0]

        def emit_k_ocs(e8, strip, lbuf, v_ws=None):
            for oc in range(4):
                og = e8 * 4 + oc
                bank = (p0, p1, p2, p3)[oc % 4]
                pl.wait("tensor", AC, max(0, pl.val(AC) - 3))
                for cc in range(32):
                    if v_ws is not None and strip == 0 and oc == 0 and cc % 8 == 0:
                        pl.wait("tensor", *v_ws[cc // 8])
                    pl.op("tensor", lambda e, bank=bank, cc=cc, oc=oc, lbuf=lbuf:
                          e.matmul(bank[:],
                                   a_w8[:, cc * 512 + oc * 128:cc * 512 + oc * 128 + 128],
                                   lbuf[:, cc * 512:(cc + 1) * 512],
                                   start=(cc == 0), stop=(cc == 31)),
                          PE if cc == 31 else None)
                v_pe = pl.val(PE)
                kv_cnt = kv_cnt_box[0]
                eb = s_ev[:, (kv_cnt % 2) * 512:(kv_cnt % 2) * 512 + 512]
                pl.wait("scalar", PE, v_pe)
                st_kv.wait_done("scalar", kv_cnt)
                v_ac = pl.op("scalar", lambda e, bank=bank, eb=eb, og=og:
                             e.activation(eb, bank[:], AF.Identity,
                                          bias=s_bk[:, og:og + 1], scale=1.0), AC)
                pl.wait("gpsimd", AC, v_ac)
                st_kv.dma("gpsimd", d_k[og * 128:(og + 1) * 128,
                                        strip * 512:(strip + 1) * 512], eb, kv_cnt)
                kv_cnt_box[0] = kv_cnt + 1

        pl.wait("scalar", CON, v_con)
        pl.wait("tensor", DV, v_bh_done)
        v_apply = None
        v_lt_pe = 0
        for strip in range(8):
            sbh = a_str[strip % 2]
            st_strip.wait_done("vector", strip)
            pl.wait("vector", PE, strip_buf_pe[strip % 2])
            for tt in range(4):
                t0 = strip * 512 + tt * 128
                if v_apply is not None:
                    pl.wait("sync", AC, v_apply)
                v_x = pl.dma("sync", a_xt1, xb[t0:t0 + 128, :], XAL)
                pl.wait("vector", XAL, v_x)
                pl.wait("scalar", XAL, v_x)
                pl.wait("scalar", PE, v_lt_pe)
                v_apply = plan_ln_tile(a_xt1, a_lt1, a_sq1)

                def dst_strip(e, cg, bank, tt=tt, sbh=sbh):
                    dst = sbh.rearrange("p (cc q) -> p cc q", q=512)[
                        :, cg * 4:(cg + 1) * 4, tt * 128:(tt + 1) * 128]
                    src = bank.rearrange("p (j q) -> p j q", q=128)
                    return e.tensor_copy(dst, src)
                plan_tok_to_fm(a_lt1, v_apply, dst_strip)
                v_lt_pe = pl.val(PE)
            v_dv = pl.val(DV)
            pl.wait("gpsimd", DV, v_dv)
            st_strip.dma("gpsimd", d_ln1[strip], sbh, strip)
            pl.wait("tensor", DV, v_dv)
            emit_k_ocs(0, strip, sbh, v_ws=v_wk0)
            strip_buf_pe[strip % 2] = pl.val(PE)
            pump_deq(2)

        def load_w_eighth(wT_src, e8):
            pl.wait("gpsimd", PE, pl.val(PE))
            vs = []
            for i in range(4):
                vs.append((W8S[i], pl.dma("gpsimd", a_w8[:, i * 4096:(i + 1) * 4096],
                                          wT_src[e8][:, i * 4096:(i + 1) * 4096],
                                          W8S[i])))
            return vs

        def load_ln1_strip(strip):
            bi = strip % 2
            pl.wait("sync", PE, strip_buf_pe[bi])
            ld_strip.dma("sync", a_str[bi][:], d_ln1[strip], strip)
            return a_str[bi], bi

        st_strip.wait_all("sync")
        pl.wait("scalar", CON, v_con)
        pl.wait("gpsimd", AC, pl.val(AC))

        for e8 in range(1, 8):
            v_ws = load_w_eighth(wkT, e8)
            for strip in range(8):
                lbuf, bi = load_ln1_strip(strip)
                pl.wait("tensor", STRL[bi], pl.val(STRL[bi]))
                emit_k_ocs(e8, strip, lbuf, v_ws=v_ws if strip == 0 else None)
                strip_buf_pe[bi] = pl.val(PE)
                pump_deq(2)

        for e8 in range(8):
            v_ws = load_w_eighth(wvT, e8)
            o0 = e8 * 512
            for strip in range(8):
                lbuf, bi = load_ln1_strip(strip)
                pl.wait("tensor", STRL[bi], pl.val(STRL[bi]))
                for ts in range(4):
                    t0 = strip * 512 + ts * 128
                    bank = (p0, p1, p2, p3)[ts % 4]
                    pl.wait("tensor", DV, max(0, pl.val(DV) - 3))
                    for cc in range(32):
                        if strip == 0 and ts == 0 and cc % 8 == 0:
                            pl.wait("tensor", *v_ws[cc // 8])
                        pl.op("tensor", lambda e, bank=bank, cc=cc, ts=ts, lbuf=lbuf:
                              e.matmul(bank[:],
                                       lbuf[:, cc * 512 + ts * 128:cc * 512 + ts * 128 + 128],
                                       a_w8[:, cc * 512:(cc + 1) * 512],
                                       start=(cc == 0), stop=False), None)
                    pl.op("tensor", lambda e, bank=bank, o0=o0:
                          e.matmul(bank[:], s_ones[0:1, :], s_bv[0:1, o0:o0 + 512],
                                   start=False, stop=True), PE)
                    v_pe = pl.val(PE)
                    kv_cnt = kv_cnt_box[0]
                    eb = s_ev[:, (kv_cnt % 2) * 512:(kv_cnt % 2) * 512 + 512]
                    pl.wait("vector", PE, v_pe)
                    st_kv.wait_done("vector", kv_cnt)
                    v_dv = pl.op("vector", lambda e, bank=bank, eb=eb:
                                 e.tensor_copy(eb, bank[:]), DV)
                    pl.wait("gpsimd", DV, v_dv)
                    st_kv.dma("gpsimd", d_v[t0:t0 + 128, o0:o0 + 512], eb, kv_cnt)
                    kv_cnt_box[0] = kv_cnt + 1
                strip_buf_pe[bi] = pl.val(PE)
                pump_deq(2)

        pl.wait("tensor", DV, v_bh_done)
        q_eb_pe = [0, 0]
        for e8 in range(8):
            v_ws = load_w_eighth(wqT, e8)
            for strip in range(2):
                lbuf, bi = load_ln1_strip(strip)
                pl.wait("tensor", STRL[bi], pl.val(STRL[bi]))
                for oc in range(4):
                    og = e8 * 4 + oc
                    head = og // 2
                    bank = (p0, p1)[oc % 2]
                    pl.wait("tensor", AC, max(0, pl.val(AC) - 1))
                    for cc in range(32):
                        if strip == 0 and oc == 0 and cc % 8 == 0:
                            pl.wait("tensor", *v_ws[cc // 8])
                        pl.op("tensor", lambda e, bank=bank, cc=cc, oc=oc, lbuf=lbuf:
                              e.matmul(bank[:],
                                       a_w8[:, cc * 512 + oc * 128:cc * 512 + oc * 128 + 128],
                                       lbuf[:, cc * 512:(cc + 1) * 512],
                                       start=(cc == 0), stop=(cc == 31)),
                              PE if cc == 31 else None)
                    v_pe = pl.val(PE)
                    eb = s_ev[:, (oc % 2) * 512:(oc % 2) * 512 + 512]
                    pl.wait("scalar", PE, v_pe)
                    pl.wait("scalar", PE, q_eb_pe[oc % 2])
                    v_ac = pl.op("scalar", lambda e, bank=bank, eb=eb, og=og:
                                 e.activation(eb, bank[:], AF.Identity,
                                              bias=s_bq[:, og:og + 1], scale=1.0), AC)
                    pl.wait("tensor", AC, v_ac)
                    v_tr = pl.op("tensor", lambda e, eb=eb, head=head:
                                 e.matmul(p4[:], s_bh[:, head * 128:(head + 1) * 128],
                                          eb, start=True, stop=True), PE)
                    q_eb_pe[oc % 2] = v_tr
                    pl.wait("vector", PE, v_tr)
                    pl.op("vector", lambda e, og=og, strip=strip:
                          e.tensor_copy(s_qm[:, og * 1024 + strip * 512:
                                             og * 1024 + (strip + 1) * 512], p4[:]),
                          DV)
                strip_buf_pe[bi] = pl.val(PE)
            pump_deq(2)
        v_qmod_done = pl.val(DV)
        if dbg:
            pl.wait("sync", DV, v_qmod_done)
            pl.dma("sync", d_qmdump, s_qm[:], X1L)

        st_kv.wait_all("sync")
        pl.wait("tensor", DV, v_qmod_done)
        v_pe_p3 = pl.val(PE)
        kh_pe = [v_pe_p3, v_pe_p3]
        vh_pe = [v_pe_p3]
        ao_cnt = 0
        v_atf_pe = 0
        v_dsum_prev = [0, 0]
        lg_busy_ac = [0, 0]
        lgevac = {}
        v_pv_evac = 0
        v_ngm_prev = [0, 0]
        kv_loads = {}

        def emit_k_load(h):
            kb = h % 2
            khbuf = a_kh[:, kb * (2 * BT):(kb + 1) * (2 * BT)]
            pl.wait("sync", PE, kh_pe[kb])
            v_k = None
            for fc in range(2):
                v_k = pl.dma("sync", khbuf[:, fc * BT:(fc + 1) * BT],
                             d_k[h * 256 + fc * 128:h * 256 + (fc + 1) * 128, :],
                             KHL2[kb])
            kv_loads[h] = ((KHL2[kb], v_k), khbuf)

        def emit_v_load(h):
            pl.wait("sync", PE, vh_pe[0])
            srcv = d_v.rearrange("(ms p) f -> p ms f", p=128)[
                :, :, h * 256:(h + 1) * 256]
            dstv = a_vh.rearrange("p (ms f) -> p ms f", f=256)
            return pl.dma("sync", dstv, srcv, VHL)

        def pass1(hq):
            h, qt = divmod(hq, 8)
            lq = qt % 2
            lg = a_lg2[lq]
            v_k, khbuf = kv_loads[h]
            if qt == 0:
                pl.wait("tensor", *v_k)
            for mt in range(8):
                bank = (p0, p1, p2, p3)[mt % 4]
                pl.wait("tensor", AC, lgevac.get(id(bank), 0))
                for fc in range(2):
                    pl.op("tensor",
                          lambda e, bank=bank, fc=fc, qt=qt, h=h, mt=mt, khbuf=khbuf:
                          e.matmul(bank[:],
                                   s_qm[:, (h * 2 + fc) * 1024 + qt * 128:
                                        (h * 2 + fc) * 1024 + qt * 128 + 128],
                                   khbuf[:, fc * BT + mt * 512:
                                         fc * BT + (mt + 1) * 512],
                                   start=(fc == 0), stop=(fc == 1)),
                          PE if fc == 1 else None)
                v_pe = pl.val(PE)
                pl.wait("scalar", PE, v_pe)
                if mt == 0:
                    pl.wait("scalar", AC, lg_busy_ac[lq])
                v_ev = pl.op("scalar", lambda e, bank=bank, mt=mt, lg=lg:
                             e.copy(lg[:, mt * 512:(mt + 1) * 512], bank[:]), AC)
                lgevac[id(bank)] = v_ev
                pl.wait("vector", AC, v_ev)
                if mt == 0:
                    pl.wait("vector", DV, v_ngm_prev[lq])
                pl.op("vector", lambda e, mt=mt, lq=lq, lg=lg: e.tensor_reduce(
                    s_mx8[:, lq * 8 + mt:lq * 8 + mt + 1],
                    lg[:, mt * 512:(mt + 1) * 512], AX.X, ALU.max, negate=True), DV)
            pl.wait("vector", DV, pl.val(DV))
            v_ngm = pl.op("vector", lambda e, lq=lq: e.tensor_reduce(
                s_ngm[:, lq:lq + 1], s_mx8[:, lq * 8:lq * 8 + 8], AX.X, ALU.min), DV)
            v_ngm_prev[lq] = v_ngm
            if qt == 7:
                kh_pe[h % 2] = pl.val(PE)
            return v_ngm

        def pass2(hq, v_ngm):
            nonlocal ao_cnt, v_atf_pe, v_pv_evac
            h, qt = divmod(hq, 8)
            lq = qt % 2
            lg = a_lg2[lq]
            if qt == 0:
                kv_loads[(h, "v")] = emit_v_load(h)
            v_v = kv_loads[(h, "v")]
            pl.wait("tensor", VHL, v_v)
            v_exps = [0] * 8
            trev_done = [0, 0]
            pv_done = [0, 0]
            tr_pe = [0] * 8

            def emit_exp(mt):
                if mt == 0:
                    pl.wait("scalar", DV, v_ngm)
                    pl.wait("scalar", DV, v_dsum_prev[lq])
                    pl.wait("scalar", PE, v_atf_pe)
                v_exps[mt] = pl.op(
                    "scalar", lambda e, lg=lg, lq=lq, mt=mt:
                    e.activation(a_atf[:, mt * 512:(mt + 1) * 512],
                                 lg[:, mt * 512:(mt + 1) * 512],
                                 AF.Exp, bias=s_ngm[:, lq:lq + 1], scale=1.0,
                                 accum_out=s_den[:, lq * 8 + mt:lq * 8 + mt + 1]),
                    AC)

            def emit_tr(mt):
                bank = (vt0, vt1)[mt % 2]
                bname = ("pt0", "pt1")[mt % 2]
                pl.wait("tensor", AC, v_exps[mt])
                pl.wait("tensor", DV, max(trev_done[mt % 2], last_evac[bname]))
                for j in range(4):
                    pl.op("tensor", lambda e, bank=bank, j=j, mt=mt:
                          e.transpose(bank[:, j * 128:(j + 1) * 128],
                                      a_atf[:, mt * 512 + j * 128:
                                            mt * 512 + (j + 1) * 128], s_idb[:]),
                          PE if j == 3 else None)
                tr_pe[mt] = pl.val(PE)
                pl.wait("vector", PE, tr_pe[mt])
                pl.wait("vector", PE, pv_done[mt % 2])
                tb = a_atT[:, (mt % 2) * 512:(mt % 2) * 512 + 512]
                v_dv = pl.op("vector", lambda e, tb=tb, bank=bank:
                             e.tensor_copy(tb, bank[:, 0:512]), DV)
                trev_done[mt % 2] = v_dv
                last_evac[bname] = v_dv
                return v_dv

            def emit_pv(mt, v_evac):
                tb = a_atT[:, (mt % 2) * 512:(mt % 2) * 512 + 512]
                pl.wait("tensor", DV, v_evac)
                if mt == 0:
                    pl.wait("tensor", DV, v_pv_evac)
                for j in range(4):
                    pl.op("tensor", lambda e, tb=tb, j=j, mt=mt:
                          e.matmul(p4[:, 0:256], tb[:, j * 128:(j + 1) * 128],
                                   a_vh[:, (mt * 4 + j) * 256:
                                        (mt * 4 + j + 1) * 256],
                                   start=(mt == 0 and j == 0),
                                   stop=(mt == 7 and j == 3)),
                          PE if j == 3 else None)
                pv_done[mt % 2] = pl.val(PE)

            emit_exp(0)
            emit_exp(1)
            evs = [0] * 8
            for mt in range(8):
                if mt + 2 < 8:
                    emit_exp(mt + 2)
                evs[mt] = emit_tr(mt)
                if mt >= 1:
                    emit_pv(mt - 1, evs[mt - 1])
            emit_pv(7, evs[7])
            v_pv_all = pl.val(PE)
            v_atf_pe = v_pv_all
            if qt == 7:
                vh_pe[0] = v_pv_all
            pl.wait("vector", AC, v_exps[7])
            v_ds = pl.op("vector", lambda e, lq=lq: e.tensor_reduce(
                s_dsum[:, lq:lq + 1], s_den[:, lq * 8:lq * 8 + 8], AX.X, ALU.add), DV)
            v_dsum_prev[lq] = v_ds
            pl.wait("vector", DV, v_ds)
            v_rc = pl.op("vector", lambda e, lq=lq:
                         e.reciprocal(s_rcp[:], s_dsum[:, lq:lq + 1]), DV)
            pl.wait("vector", DV, v_rc)
            pl.wait("vector", PE, v_pv_all)
            aob = a_ao[:, (ao_cnt % 2) * HD:(ao_cnt % 2) * HD + HD]
            st_ao.wait_done("vector", ao_cnt)
            v_pv_evac = pl.op("vector", lambda e, aob=aob: e.tensor_scalar(
                aob, p4[:, 0:256], s_rcp[:, 0:1], None, ALU.mult), DV)
            pl.wait("scalar", DV, v_pv_evac)
            st_ao.dma("scalar", d_attn[qt * 128:(qt + 1) * 128,
                                       h * 256:(h + 1) * 256], aob, ao_cnt)
            ao_cnt += 1

        emit_k_load(0)
        ngms = {0: pass1(0)}
        for hq in range(NH * 8):
            h2 = (hq + 2) // 8
            if h2 < NH and h2 not in kv_loads:
                emit_k_load(h2)
            if hq + 1 < NH * 8:
                ngms[hq + 1] = pass1(hq + 1)
            pass2(hq, ngms[hq])
            pump_deq(2, eng="alt")

        st_ao.wait_all("sync")
        pl.wait("sync", AC, pl.val(AC))
        v_pe_p4 = pl.val(PE)
        v_ac_p4 = pl.val(AC)
        blocks = [list(range(NG_D))[i:i + 4] for i in range(0, NG_D, 4)]
        wait_deq_upto(4 * 4 - 1)
        wt0 = a_wt[0][:, 0:32 * 512]
        v_wt0 = None
        for cc in range(32):
            v_wt0 = ld_wt.dma("sync", wt0[:, cc * 512:(cc + 1) * 512],
                              d_wg[0:512, cc * 128:(cc + 1) * 128], 0,
                              transpose=True)
        v_apply2 = None
        v_add1 = 0
        for tt in range(8):
            t0 = tt * 128
            pl.wait("sync", PE, pl.val(PE))
            if v_apply2 is not None:
                pl.wait("sync", AC, v_apply2)
                pl.wait("sync", DV, v_add1)
            pl.dma("sync", a_lt4, d_attn[t0:t0 + 128, :], XAL)
            v_x = pl.dma("sync", a_xt4, xb[t0:t0 + 128, :], XAL)
            pl.wait("vector", XAL, v_x)
            pl.wait("scalar", XAL, v_x)
            v_apply = plan_ln_tile(a_lt4, a_lt4b, a_sq4)
            pl.wait("vector", AC, v_apply)
            st_x1.wait_done("vector", tt)
            v_add1 = pl.op("vector", lambda e: e.tensor_add(a_x14, a_xt4, a_lt4b), DV)
            pl.wait("gpsimd", DV, v_add1)
            st_x1.dma("gpsimd", d_x1[t0:t0 + 128, :], a_x14, tt)
            pl.wait("vector", DV, v_add1)
            pl.wait("scalar", DV, v_add1)
            v_apply2 = plan_ln_tile(a_x14, a_lt4, a_sq4)

            def dst_qm(e, cg, bank, tt=tt):
                dst = s_qm.rearrange("p (cc q) -> p cc q", q=1024)[
                    :, cg * 4:(cg + 1) * 4, tt * 128:(tt + 1) * 128]
                src = bank.rearrange("p (j q) -> p j q", q=128)
                return e.tensor_copy(dst, src)
            plan_tok_to_fm(a_lt4, v_apply2, dst_qm)
            pump_deq(2)
            if tt == 3:
                v_ln2_half = pl.val(DV)
        v_ln2_done = pl.val(DV)
        if dbg:
            pl.wait("sync", DV, v_ln2_done)
            pl.dma("sync", d_ln2dump, s_qm[:], X1L)

        pl.wait("sync", AC, pl.val(AC))
        pl.wait("sync", DV, pl.val(DV))
        v_pe_p5 = pl.val(PE)
        wq_pe = [v_pe_p5, v_pe_p5]
        h_cnt = 0
        for bi2, blk in enumerate(blocks):
            nbo = len(blk) * 128
            for which, wsrc in ((0, d_wg), (1, d_wu)):
                wb = which
                wtile = a_wt[wb][:, 0:32 * nbo]
                if bi2 == 0 and which == 0:
                    pass
                else:
                    wait_deq_upto(min((blk[-1] + 1) * 4, n_gu_slabs) - 1)
                    pl.wait("sync", PE, wq_pe[wb])
                    for cc in range(32):
                        src = wsrc[blk[0] * 128: blk[0] * 128 + nbo,
                                   cc * 128:(cc + 1) * 128]
                        ld_wt.dma("sync", wtile[:, cc * nbo:(cc + 1) * nbo], src,
                                  wb, transpose=True)
                pl.wait("tensor", WTL[wb], pl.val(WTL[wb]))
                if which == 0:
                    pl.wait("scalar", DV, pl.val(DV))
                for j, oc in enumerate(blk):
                    for t2 in range(2):
                        bank = (p0, p1, p2, p3)[(j * 2 + t2) % 4]
                        if bi2 == 0 and which == 0:
                            pl.wait("tensor", DV,
                                    v_ln2_half if t2 == 0 else v_ln2_done)
                        elif bi2 == 0 and j == 0 and t2 == 0:
                            pl.wait("tensor", DV, v_ln2_done)
                        if which == 0:
                            pl.wait("tensor", AC, max(0, pl.val(AC) - 3))
                        else:
                            pl.wait("tensor", DV, max(0, pl.val(DV) - 3))
                        for cc in range(32):
                            pl.op("tensor",
                                  lambda e, bank=bank, cc=cc, j=j, t2=t2,
                                  wtile=wtile, nbo=nbo:
                                  e.matmul(bank[:],
                                           wtile[:, cc * nbo + j * 128:
                                                 cc * nbo + (j + 1) * 128],
                                           s_qm[:, cc * 1024 + t2 * 512:
                                                cc * 1024 + (t2 + 1) * 512],
                                           start=(cc == 0), stop=(cc == 31)),
                                  PE if cc == 31 else None)
                        v_pe = pl.val(PE)
                        if which == 0:
                            pl.wait("scalar", PE, v_pe)
                            pl.op("scalar", lambda e, bank=bank, j=j, t2=t2:
                                  e.activation(a_gate[:, j * 1024 + t2 * 512:
                                                      j * 1024 + (t2 + 1) * 512],
                                               bank[:], AF.Silu), AC)
                        else:
                            pl.wait("vector", PE, v_pe)
                            pl.wait("vector", AC, pl.val(AC))
                            st_h.wait_done("vector", h_cnt)
                            pl.op("vector",
                                  lambda e, bank=bank, j=j, t2=t2, hc=h_cnt:
                                  e.tensor_mul(a_h5[:, (hc % 2) * 1024 + t2 * 512:
                                                    (hc % 2) * 1024 + (t2 + 1) * 512],
                                               bank[:],
                                               a_gate[:, j * 1024 + t2 * 512:
                                                      j * 1024 + (t2 + 1) * 512]),
                                  DV)
                    if which == 1:
                        pl.wait("gpsimd", DV, pl.val(DV))
                        hb = a_h5[:, (h_cnt % 2) * 1024:(h_cnt % 2) * 1024 + 1024]
                        st_h.dma("gpsimd", d_h[oc * 128:(oc + 1) * 128, :], hb, h_cnt)
                        h_cnt += 1
                wq_pe[wb] = pl.val(PE)
            pump_deq(10)

        pump_deq(len(slabs))
        st_dq.wait_all("sync")
        st_h.wait_all("sync")
        banks8 = (p0, p1, p2, p3, p4, p5, pt0, pt1)
        v_pe_p6 = pl.val(PE)
        hl_pe = [v_pe_p6] * 4
        v_dv_p6 = pl.val(DV)
        for sl in range(8):
            for ic in range(NG_D):
                b = ic % 4
                hb = a_h2[:, b * 1024:b * 1024 + 1024]
                rb = a_rhs[:, b * 512:b * 512 + 512]
                pl.wait("sync", PE, hl_pe[b])
                ld_h.dma("sync", hb, d_h[ic * 128:(ic + 1) * 128, :], ic)
                ld_r.dma("sync", rb,
                         d_wd[sl * 512:(sl + 1) * 512, ic * 128:(ic + 1) * 128],
                         ic, transpose=True)
                pl.wait("tensor", HL[b], pl.val(HL[b]))
                pl.wait("tensor", RL[b], pl.val(RL[b]))
                if ic == 0:
                    pl.wait("tensor", DV, v_dv_p6)
                for tt in range(8):
                    pl.op("tensor", lambda e, tt=tt, hb=hb, rb=rb, ic=ic:
                          e.matmul(banks8[tt][:],
                                   hb[:, tt * 128:(tt + 1) * 128], rb,
                                   start=(ic == 0), stop=(ic == NG_D - 1)),
                          PE if tt == 7 else None)
                hl_pe[b] = pl.val(PE)
            v_pe = pl.val(PE)
            pl.wait("vector", PE, v_pe)
            for tt in range(8):
                pl.op("vector", lambda e, tt=tt, sl=sl:
                      e.tensor_copy(s_qm[:, tt * 4096 + sl * 512:
                                         tt * 4096 + (sl + 1) * 512],
                                    banks8[tt][:]), DV)
            v_dv_p6 = pl.val(DV)
        v_down_done = pl.val(DV)

        pl.wait("scalar", DV, v_down_done)
        pl.wait("vector", DV, v_down_done)
        pl.wait("scalar", PE, pl.val(PE))
        pl.wait("sync", PE, pl.val(PE))
        v_apply3 = None
        v_add_prev = 0
        n_out = 0
        for tt in range(8):
            t0 = tt * 128
            dtile = s_qm[:, tt * 4096:(tt + 1) * 4096]
            pl.wait("sync", DV, v_add_prev)
            v_x1 = pl.dma("sync", a_x17, d_x1[t0:t0 + 128, :], X1L)
            if v_apply3 is not None:
                pl.wait("vector", OS, 16 * tt)
                pl.wait("scalar", DV, v_add_prev)
            v_apply3 = plan_ln_tile(dtile, a_lt7, a_sq7)
            pl.wait("vector", AC, v_apply3)
            pl.wait("vector", X1L, v_x1)
            v_add_prev = pl.op("vector",
                               lambda e: e.tensor_add(a_ot7, a_x17, a_lt7), DV)
            pl.wait("sync", DV, v_add_prev)
            pl.dma("sync", out[t0:t0 + 128, :], a_ot7, OS)
            n_out += 1
        pl.wait("sync", OS, 16 * n_out)
        pl.wait("gpsimd", OS, 16 * n_out)

        if dbg:
            DGS = X1L
            for nm, src in (("dbg_ln1", d_ln1), ("dbg_k", d_k), ("dbg_v", d_v),
                            ("dbg_attn", d_attn), ("dbg_x1", d_x1), ("dbg_h", d_h),
                            ("dbg_wg", d_wg), ("dbg_wd", d_wd)):
                pl.dma("sync", dbg_out[nm], src, DGS)
            pl.dma("sync", dbg_out["dbg_qm"], d_qmdump, DGS)
            pl.dma("sync", dbg_out["dbg_ln2"], d_ln2dump, DGS)
            pl.dma("sync", dbg_out["dbg_bh"], s_bh[:], DGS)
            pl.dma("sync", dbg_out["dbg_wkt"], s_wkT[:], DGS)
            pl.wait("sync", DGS, pl.val(DGS))

        block = ctx.enter_context(nc.Block())
        for name in ENGINES:
            fns = pl.ops[name]
            getattr(block, name)(lambda e, fns=fns: [fn(e) for fn in fns])

    return nc


def make_in_maps(x, Wq, bq, Wk, bk, Wv, bv, bias_weight,
                 gqw, gsc, gze, uqw, usc, uze, dqw, dsc, dze):
    def c(a):
        return np.ascontiguousarray(a)

    def as_i16(q):
        return c(np.asarray(q, dtype=np.int32).view(np.int16)[:, ::2])

    def tile8(W):
        wt = np.asarray(W, np.float32).T.reshape(32, 128, 8, 512)
        return c(wt.transpose(2, 1, 0, 3).reshape(8, 128, 32 * 512))

    common = dict(
        wqT=tile8(Wq), wkT=tile8(Wk), wvT=tile8(Wv),
        bq2=c(np.asarray(bq, np.float32).reshape(32, 128).T),
        bk2=c(np.asarray(bk, np.float32).reshape(32, 128).T),
        bv1=c(np.asarray(bv, np.float32).reshape(1, DM)),
        bwT=c(np.asarray(bias_weight, np.float32).T),
        gq=as_i16(gqw), uq=as_i16(uqw), dq=as_i16(dqw),
        gz=c(np.asarray(gze, np.float32)), gs=c(np.asarray(gsc, np.float32)),
        uz=c(np.asarray(uze, np.float32)), us=c(np.asarray(usc, np.float32)),
        dz=c(np.asarray(dze, np.float32)), ds=c(np.asarray(dsc, np.float32)),
        **{k: c(v) for k, v in host_constants().items()})
    in_maps = []
    for core in range(8):
        b, q = core // 4, core % 4
        xb_b = np.asarray(x[b], np.float32)
        perm = np.concatenate([xb_b[q * QT:(q + 1) * QT],
                               xb_b[:q * QT], xb_b[(q + 1) * QT:]], axis=0)
        in_maps.append(dict(common, xb=c(perm)))
    return in_maps


def kernel(L0_attn_x, L0_attn_Wq, L0_attn_bq, L0_attn_Wk, L0_attn_bk,
           L0_attn_Wv, L0_attn_bv, L0_attn_bias_weight,
           L0_ffn_gate_qweight, L0_ffn_gate_scales, L0_ffn_gate_zeros,
           L0_ffn_up_qweight, L0_ffn_up_scales, L0_ffn_up_zeros,
           L0_ffn_down_qweight, L0_ffn_down_scales, L0_ffn_down_zeros):
    x = np.asarray(L0_attn_x, dtype=np.float32)
    assert x.shape == (2, BT, DM)
    in_maps = make_in_maps(x, L0_attn_Wq, L0_attn_bq, L0_attn_Wk, L0_attn_bk,
                           L0_attn_Wv, L0_attn_bv, L0_attn_bias_weight,
                           L0_ffn_gate_qweight, L0_ffn_gate_scales,
                           L0_ffn_gate_zeros, L0_ffn_up_qweight,
                           L0_ffn_up_scales, L0_ffn_up_zeros,
                           L0_ffn_down_qweight, L0_ffn_down_scales,
                           L0_ffn_down_zeros)
    nc = build_program()
    res = run_bass_kernel_spmd(nc, in_maps, core_ids=list(range(8)))
    outp = np.empty((2, BT, DM), np.float32)
    for core in range(8):
        b, q = core // 4, core % 4
        outp[b, q * QT:(q + 1) * QT] = res.results[core]["out"]
    return outp
